# version 12
# baseline (speedup 1.0000x reference)
import numpy as np
import ml_dtypes
import concourse.bass as bass
import concourse.mybir as mybir
from concourse.bass_utils import run_bass_kernel_spmd

F32 = mybir.dt.float32
BF16 = mybir.dt.bfloat16
I8 = mybir.dt.int8
AF = mybir.ActivationFunctionType
ALU = mybir.AluOpType

T = 1024
D = 2048
NDC = 16
NOG = 4
NTH = 2
NTC = 8
NJC = 8
NER = 4

EPS_LN = 1e-5
EPS_F32 = float(np.finfo(np.float32).eps)

STAGES = ("ffn1", "attn", "ffn2")
USE_CC = True
ATTN_UPTO = 9

BASE = 16512
R1 = BASE
R2 = BASE + 65536
R3 = BASE + 98304
R4 = BASE + 131072
R5 = BASE + 163840
H = BASE + 196608
SBUF_TOP = 229344


class B:

    def __init__(self, nc):
        self.nc = nc
        self.cur = None
        self.e = None
        self.sems = {}
        self.cnt = {}

    def sem_init(self, names, es):
        for n in names:
            self.sems[n] = es.enter_context(self.nc.semaphore(n))

    def begin_pass(self, eng, handle):
        self.cur = eng
        self.e = handle
        for n in self.sems:
            self.cnt[n] = 0
        self.hwm = {}

    def w(self, eng, sname, val):
        if val is None or val <= 0:
            return
        key = (eng, sname)
        if self.hwm.get(key, -1) >= val:
            return
        self.hwm[key] = val
        if self.cur == eng:
            self.e.wait_ge(self.sems[sname], val)

    def do(self, eng, fn, incs=()):
        out = {}
        if self.cur == eng:
            ins = fn(self.e)
            for s, n in incs:
                ins = ins.then_inc(self.sems[s], n)
        for s, n in incs:
            self.cnt[s] += n
            out[s] = self.cnt[s]
        return out

    def c(self, sname):
        return self.cnt[sname]

    def barrier(self):
        snap = {s: self.cnt[s] for s in ("pe", "dve", "act", "pool")}
        m = {"tensor": "pe", "vector": "dve", "scalar": "act", "gpsimd": "pool"}
        for eng, own in m.items():
            for o_sem in snap:
                if o_sem != own:
                    self.w(eng, o_sem, snap[o_sem])
        for o_sem in snap:
            self.w("sync", o_sem, snap[o_sem])


class Pool:

    def __init__(self, b, banks):
        self.b = b
        self.banks = [{"t": t, "rel": []} for t in banks]
        self.idx = 0

    def reset(self):
        self.idx = 0
        for bk in self.banks:
            bk["rel"] = []

    def acquire(self):
        bk = self.banks[self.idx % len(self.banks)]
        self.idx += 1
        for sem, cntv in bk["rel"]:
            self.b.w("tensor", sem, cntv)
        return bk

    def release(self, bk, *sems):
        bk["rel"] = [(s, self.b.cnt[s]) for s in sems]


def build_program(stages, use_cc=None):
    global USE_CC
    if use_cc is not None:
        USE_CC = use_cc
    nc = bass.Bass("TRN2", target_bir_lowering=False, debug=False)

    xT = nc.dram_tensor("xT", [D, T], F32, kind="ExternalInput")
    wqT = nc.dram_tensor("wqT", [16, 128, 16, 128], F32, kind="ExternalInput")
    wkT = nc.dram_tensor("wkT", [16, 128, 16, 128], F32, kind="ExternalInput")
    wvT = nc.dram_tensor("wvT", [16, 128, 16, 128], F32, kind="ExternalInput")
    wgT = nc.dram_tensor("wgT", [16, 128, 16, 128], F32, kind="ExternalInput")
    wpT = nc.dram_tensor("wpT", [16, 128, 16, 128], F32, kind="ExternalInput")
    wgate = nc.dram_tensor("wgate", [64, 128, 16, 128], I8, kind="ExternalInput")
    wup = nc.dram_tensor("wup", [64, 128, 16, 128], I8, kind="ExternalInput")
    wdown = nc.dram_tensor("wdown", [8, 16, 128, 8, 128], I8, kind="ExternalInput")
    sgate = nc.dram_tensor("sgate", [128, 64], F32, kind="ExternalInput")
    sup = nc.dram_tensor("sup", [128, 64], F32, kind="ExternalInput")
    sdown = nc.dram_tensor("sdown", [128, 16], F32, kind="ExternalInput")
    ident_in = nc.dram_tensor("ident", [128, 128], BF16, kind="ExternalInput")
    mask_in = nc.dram_tensor("mask", [128, 128], F32, kind="ExternalInput")
    bflag_in = nc.dram_tensor("bflag", [128, 1], F32, kind="ExternalInput")
    aflag_in = nc.dram_tensor("aflag", [128, 1], F32, kind="ExternalInput")
    xout = nc.dram_tensor("xout", [D, T], F32, kind="ExternalOutput")

    park = nc.dram_tensor("park", [D, T], F32)
    kvc = nc.dram_tensor("kvc", [D, D], BF16)
    kvs = nc.dram_tensor("kvs", [D, D], BF16)

    A = lambda name, shape, dt, off: nc.alloc_sbuf_tensor_at(name, shape, dt, offset=off)
    x = A("x", [128, NDC * T], F32, R1)
    pstage32 = [A("pstage32a", [128, 2048], F32, R5),
                A("pstage32b", [128, 2048], F32, R5 + 16384)]
    pstageb = [A(f"pstageb{i}", [128, 2048], BF16, R5 + 8192 + i * 4096) for i in range(2)]
    v_tm = A("v_tm", [128, NJC * D], BF16, R1)
    g_tm = A("g_tm", [128, NTC * D], BF16, R1 + 32768)
    xn = A("xn", [128, NDC * T], BF16, R2)
    s_t = A("s_t", [128, NJC * T], BF16, R2)
    og_st = [A(f"og_st{i}", [128, 512], BF16, H + 12320 + i * 1024) for i in range(2)]
    s_half = A("s_half", [128, 1], F32, H + 14368)
    ones_cb = A("ones_cb", [128, 1], BF16, H + 14400)
    wp32 = [A("wp32a", [128, 2048], F32, R2),
            A("wp32b", [128, 2048], F32, R2 + 16384)]
    wpb = [A(f"wpb{i}", [128, 2048], BF16, R2 + 8192 + i * 4096) for i in range(2)]
    fuse = A("fuse", [128, 16 * T], BF16, R3)
    kbuf = A("kbuf", [128, NDC * T], BF16, R3)
    zbuf = [A(f"zbuf{i}", [128, 16 * 512], BF16, R3 + i * 16384) for i in range(2)]
    vbuf = A("vbuf", [128, NDC * T], BF16, R4)
    wd8 = [A(f"wd8{i}", [128, 2048], I8, R4 + i * 2048) for i in range(2)]
    wdb = [A(f"wdb{i}", [128, 2048], BF16, R4 + 4096 + i * 4096) for i in range(2)]
    gsl = [A(f"gsl{i}", [128, 512], BF16, R4 + 12288 + i * 1024) for i in range(2)]
    ubuf = [A(f"ubuf{i}", [128, 512], BF16, R4 + 14336 + i * 1024) for i in range(2)]
    lnsig = [A(f"lnsig{i}", [128, 512], F32, R4 + 4096 + i * 2048) for i in range(2)]
    atmp = [A(f"atmp{i}", [128, 512], F32, R4 + 12288 + i * 2048) for i in range(2)]
    bstage = A("bstage", [128, 16 * 512], BF16, R4)
    qg32 = [A("qg32a", [128, 2048], F32, R4 + 16384),
            A("qg32b", [128, 2048], F32, R4 + 8192)]
    datmp = [A(f"datmp{i}", [128, 512], F32, R4 + 16384 + i * 2048) for i in range(2)]
    qgb = [A(f"qgb{i}", [128, 2048], BF16, R4 + 24576 + i * 4096) for i in range(2)]
    ogfm = A("ogfm", [128, 16 * T], BF16, R4)
    wg8 = A("wg8", [128, 2048], I8, R5)
    wu8 = A("wu8", [128, 2048], I8, R5 + 2048)
    wgbf = [A(f"wgbf{i}", [128, 2048], BF16, R5 + 4096 + i * 4096) for i in range(2)]
    wubf = [A(f"wubf{i}", [128, 2048], BF16, R5 + 12288 + i * 4096) for i in range(2)]
    k_tm = A("k_tm", [128, NJC * D], BF16, R5)
    qbuf = A("qbuf", [128, NDC * T], BF16, R5)
    sq = [A(f"sq{i}", [128, 512], BF16, H + i * 2048) for i in range(2)]
    row_m = A("row_m", [1, 512], F32, H + 4096)
    row_s = A("row_s", [1, 512], F32, H + 6144)
    row_ms = A("row_ms", [1, 512], F32, H + 8192)
    s_sg = A("s_sg", [128, 64], F32, H + 10240)
    s_su = A("s_su", [128, 64], F32, H + 10496)
    s_sdh = A("s_sdh", [128, 16], F32, H + 10752)
    s_id = A("s_id", [128, 128], BF16, H + 10816)
    s_mask = A("s_mask", [128, 128], F32, H + 11072)
    s_bf = A("s_bf", [128, 1], F32, H + 11584)
    s_af = A("s_af", [128, 1], F32, H + 11616)
    ones_c = A("ones_c", [128, 1], F32, H + 11648)
    ones_r = A("ones_r", [1, 128], F32, H + 11680)
    c_e1b = A("c_e1b", [1, 1], F32, H + 12192)
    c_e1s = A("c_e1s", [1, 1], F32, H + 12224)
    c_e2b = A("c_e2b", [1, 1], F32, H + 12256)
    c_e2s = A("c_e2s", [1, 1], F32, H + 12288)
    assert H + 14432 <= SBUF_TOP

    from contextlib import ExitStack
    es = ExitStack()
    with es:
        pA = es.enter_context(nc.psum_tensor("pA", [128, 512], F32))
        pB = es.enter_context(nc.psum_tensor("pB", [128, 512], F32))
        pC = es.enter_context(nc.psum_tensor("pC", [128, 512], F32))
        pD = es.enter_context(nc.psum_tensor("pD", [128, 512], F32))
        pT0 = es.enter_context(nc.psum_tensor("pT0", [128, 1024], BF16))
        pT1 = es.enter_context(nc.psum_tensor("pT1", [128, 1024], BF16))
        pT0f = pT0.bitcast(F32)
        pT1f = pT1.bitcast(F32)
        pR0 = es.enter_context(nc.psum_tensor("pR0", [1, 512], F32))
        pR1 = es.enter_context(nc.psum_tensor("pR1", [1, 512], F32))

        b = B(nc)
        b.sem_init(["s_pre", "s_x", "s_x2", "s_wg", "s_wu", "s_wd0", "s_wd1", "s_ps", "s_ps1",
                    "s_park", "s_bo", "s_z0", "s_z1", "s_xla", "s_xlb",
                    "pe", "dve", "act", "pool", "cc", "st"], es)
        pool = Pool(b, [pA, pB, pC, pD])
        tbank0 = {"t": pT0, "tf": pT0f, "rel": []}
        tbank1 = {"t": pT1, "tf": pT1f, "rel": []}
        ptb = [tbank0, tbank1]
        pool6 = Pool(b, [])
        pool6.banks = pool.banks + [tbank0, tbank1]
        prow = {"R0": [], "R1": []}

        def tb_wait(tb_):
            for sem, cntv in tb_["rel"]:
                b.w("tensor", sem, cntv)

        def tb_release(tb_, *sems):
            tb_["rel"] = [(s, b.cnt[s]) for s in sems]

        blk = es.enter_context(nc.Block())

        def xs(dc, th):
            return x[:, dc * T + th * 512: dc * T + th * 512 + 512]

        sq_mm_hist = {}
        xb_hist = {}
        atmp_hist = {}

        def emit_ln(b, kind, x_ready=None):
            bias_sl = c_e2b if kind == 2 else c_e1b
            scale_sl = c_e2s if kind == 2 else c_e1s
            apply_done = {}
            tgt = {}
            for th in range(NTH):
                if th == 0:
                    for sem, cntv in prow["R0"] + prow["R1"]:
                        b.w("tensor", sem, cntv)
                    tgt[0] = [None, None, pR0, pR1, 0]
                else:
                    bk1 = pool.acquire()
                    bk2 = pool.acquire()
                    tgt[1] = [bk1, bk2, bk1["t"], bk2["t"], 0]
                for dc in range(NDC):
                    if x_ready:
                        b.w("gpsimd", *x_ready[(dc, th)])
                        b.w("scalar", *x_ready[(dc, th)])
                    b.w("gpsimd", "pe", xb_hist.get(dc % 2))
                    rc = b.do("gpsimd", lambda e, dc=dc, th=th: e.tensor_copy(
                        og_st[dc % 2][:, :], xs(dc, th)), [("pool", 1)])
                    b.w("scalar", "pe", sq_mm_hist.get(dc % 2))
                    b.do("scalar", lambda e, dc=dc, th=th: e.activation(
                        sq[dc % 2][:, :], xs(dc, th), AF.Square), [("act", 1)])
                    b.w("tensor", "pool", rc["pool"])
                    r1 = b.do("tensor", lambda e, t=tgt[th][2], dc=dc, st=(dc == 0),
                              sp=(dc == NDC - 1): e.matmul(
                                  t[:1, :], ones_cb[:, :1], og_st[dc % 2][:, :],
                                  start=st, stop=sp), [("pe", 1)])
                    xb_hist[dc % 2] = r1["pe"]
                    b.w("tensor", "act", b.c("act"))
                    r2 = b.do("tensor", lambda e, t=tgt[th][3], dc=dc, st=(dc == 0),
                              sp=(dc == NDC - 1): e.matmul(
                                  t[:1, :], ones_cb[:, :1], sq[dc % 2][:, :],
                                  start=st, stop=sp), [("pe", 1)])
                    sq_mm_hist[dc % 2] = r2["pe"]
                tgt[th][4] = b.c("pe")
            for th in range(NTH):
                bk1, bk2, tS, tQ, stats_pe = tgt[th]
                b.w("vector", "pe", stats_pe)
                b.do("vector", lambda e, tS=tS: e.tensor_scalar_mul(
                    row_m[:, :], tS[:1, :], 1.0 / D), [("dve", 1)])
                if th == 0:
                    prow["R0"] = [("dve", b.cnt["dve"])]
                b.do("vector", lambda e, tQ=tQ: e.tensor_scalar_mul(
                    row_s[:, :], tQ[:1, :], 1.0 / D), [("dve", 1)])
                if th == 0:
                    prow["R1"] = [("dve", b.cnt["dve"])]
                else:
                    pool.release(bk1, "dve")
                    pool.release(bk2, "dve")
                b.w("vector", "dve", b.c("dve"))
                b.do("vector", lambda e: e.tensor_mul(row_ms[:, :], row_m[:, :], row_m[:, :]),
                     [("dve", 1)])
                b.w("vector", "dve", b.c("dve"))
                b.do("vector", lambda e: e.tensor_sub(row_s[:, :], row_s[:, :], row_ms[:, :]),
                     [("dve", 1)])
                b.w("scalar", "dve", b.c("dve"))
                b.do("scalar", lambda e: e.activation(
                    row_ms[:, :], row_s[:, :], AF.Sqrt, bias=bias_sl[:1, :1],
                    scale=scale_sl[:1, :1]), [("act", 1)])
                b.w("vector", "act", b.c("act"))
                b.do("vector", lambda e: e.reciprocal(row_s[:, :], row_ms[:, :]), [("dve", 1)])
                b.w("vector", "dve", b.c("dve"))
                b.do("vector", lambda e: e.tensor_mul(row_ms[:, :], row_m[:, :], row_s[:, :]),
                     [("dve", 1)])
                bkS = pool.acquire()
                b.w("tensor", "dve", b.c("dve"))
                b.do("tensor", lambda e, t=bkS["t"]: e.matmul(
                    t[:, :], ones_r[:1, :], row_s[:1, :], start=True, stop=True), [("pe", 1)])
                bkM = pool.acquire()
                b.do("tensor", lambda e, t=bkM["t"]: e.matmul(
                    t[:, :], ones_r[:1, :], row_ms[:1, :], start=True, stop=True), [("pe", 1)])
                b.w("scalar", "pe", b.c("pe"))
                b.do("scalar", lambda e, th=th, t=bkS["t"]: e.activation(
                    lnsig[th % 2][:, :], t[:, :], AF.Copy), [("act", 1)])
                pool.release(bkS, "act")
                sig_ready = b.c("act")
                b.w("gpsimd", "act", sig_ready)
                b.w("vector", "pe", b.c("pe"))
                for dc in range(NDC):
                    if x_ready:
                        b.w("gpsimd", *x_ready[(dc, th)])
                    b.w("gpsimd", "dve", atmp_hist.get(dc % 2))
                    r1 = b.do("gpsimd", lambda e, dc=dc, th=th: e.tensor_mul(
                        atmp[dc % 2][:, :], xs(dc, th), lnsig[th % 2][:, :]), [("pool", 1)])
                    b.w("vector", "pool", r1["pool"])
                    r2 = b.do("vector", lambda e, dc=dc, th=th, m=bkM["t"]:
                              e.tensor_sub(xn[:, dc * T + th * 512: dc * T + th * 512 + 512],
                                           atmp[dc % 2][:, :], m[:, :]), [("dve", 1)])
                    atmp_hist[dc % 2] = r2["dve"]
                apply_done[th] = b.c("dve")
                pool.release(bkM, "dve")
            return apply_done

        def ffn_gu_dma(b, st, oc):
            if oc > 0:
                b.w("sync", "dve", st["gu_cast"][oc - 1])
            b.do("sync", lambda e, oc=oc: e.dma_start(
                out=wg8[:, :],
                in_=wgate.ap()[oc, :, :, :].rearrange("p hc w -> p (hc w)")),
                [("s_wg", 16)])
            b.do("sync", lambda e, oc=oc: e.dma_start(
                out=wu8[:, :],
                in_=wup.ap()[oc, :, :, :].rearrange("p hc w -> p (hc w)")),
                [("s_wu", 16)])
            st["gu_dmad"][oc] = (b.c("s_wg"), b.c("s_wu"))

        def ffn_gu_cast(b, st, oc):
            slot = oc % 2
            b.w("vector", "s_wg", st["gu_dmad"][oc][0])
            b.w("vector", "pe", st["slot_mm"].get(("g", slot)))
            b.do("vector", lambda e, s=slot: e.tensor_copy(wgbf[s][:, :], wg8[:, :]),
                 [("dve", 1)])
            b.w("vector", "s_wu", st["gu_dmad"][oc][1])
            b.w("vector", "pe", st["slot_mm"].get(("u", slot)))
            b.do("vector", lambda e, s=slot: e.tensor_copy(wubf[s][:, :], wu8[:, :]),
                 [("dve", 1)])
            st["gu_cast"][oc] = b.c("dve")

        def ffn_wd_dma(b, st, pi):
            og, k = pi // 8, pi % 8
            ds = pi % 2
            if pi >= 2:
                b.w("sync", "dve", st["wd_cast"][pi - 2])
            b.do("sync", lambda e, og=og, k=k, ds=ds: e.dma_start(
                out=wd8[ds].ap().rearrange("p (dcl ol w) -> p dcl ol w", dcl=2, ol=8),
                in_=wdown.ap()[og, 2 * k: 2 * k + 2, :, :, :].rearrange(
                    "dcl p ol w -> p dcl ol w")), [(f"s_wd{ds}", 16)])
            st["wd_dmad"][pi] = b.c(f"s_wd{ds}")

        def ffn_wd_cast(b, st, pi):
            ds = pi % 2
            b.w("vector", f"s_wd{ds}", st["wd_dmad"][pi])
            b.w("vector", "pe", st["slot_mm"].get(("d", ds)))
            b.do("vector", lambda e, ds=ds: e.tensor_copy(wdb[ds][:, :], wd8[ds][:, :]),
                 [("dve", 1)])
            st["wd_cast"][pi] = b.c("dve")

        def ffn_prolog(b):
            st = {"gu_cast": {}, "gu_dmad": {}, "slot_mm": {}, "wd_cast": {},
                  "wd_dmad": {}, "down_evac": {}, "down_pe": {}, "fuse_done": {},
                  "datmp_hist": {}}
            b.w("sync", "dve", b.cnt["dve"])
            b.w("sync", "pe", b.cnt["pe"])
            b.w("vector", "pe", b.cnt["pe"])
            ffn_gu_dma(b, st, 0)
            ffn_gu_cast(b, st, 0)
            ffn_wd_dma(b, st, 0)
            ffn_wd_dma(b, st, 1)
            return st

        def emit_ffn(b, st, apply_done):
            slot_mm = st["slot_mm"]
            fuse_done = st["fuse_done"]
            datmp_hist = st["datmp_hist"]
            fuse_dve = None
            wd_next = 2
            pool_fence = b.c("pool")
            for og in range(9):
                for ol in range(8):
                    if og < 8:
                        oc = og * 8 + ol
                        slot = oc % 2
                        fsl = (og % 2) * 8 + ol
                        if oc + 1 < 64:
                            ffn_gu_dma(b, st, oc + 1)
                        b.w("tensor", "dve", st["gu_cast"][oc])
                        gbanks, ubanks = [], []
                        for wb, banks, tag in ((wgbf, gbanks, "g"), (wubf, ubanks, "u")):
                            for th in range(NTH):
                                bk = pool6.acquire()
                                if oc == 0 and apply_done:
                                    b.w("tensor", "dve", apply_done[th])
                                for hc in range(NDC):
                                    b.do("tensor", lambda e, t=bk.get("tf", bk["t"]), wb=wb, s=slot,
                                         hc=hc, th=th, st_=(hc == 0),
                                         sp=(hc == NDC - 1): e.matmul(
                                             t[:, :], wb[s][:, hc * 128:(hc + 1) * 128],
                                             xn[:, hc * T + th * 512:
                                                hc * T + th * 512 + 512],
                                             start=st_, stop=sp),
                                         [("pe", 1)] if hc == NDC - 1 else [])
                                bk["done"] = b.c("pe")
                                banks.append(bk)
                            slot_mm[(tag, slot)] = b.c("pe")
                        if oc + 1 < 64:
                            ffn_gu_cast(b, st, oc + 1)
                        for th in range(NTH):
                            gb, ub = gbanks[th], ubanks[th]
                            if oc == 0:
                                b.w("scalar", "pool", pool_fence)
                                if apply_done:
                                    b.w("scalar", "dve", max(apply_done.values()))
                            if fuse_dve is not None:
                                b.w("scalar", "dve", fuse_dve)
                            b.w("scalar", "pe", gb["done"])
                            b.do("scalar", lambda e, t=gb.get("tf", gb["t"]), th=th, oc=oc: e.activation(
                                gsl[th][:, :], t[:, :], AF.Silu, scale=s_sg[:, oc:oc + 1]),
                                [("act", 1)])
                            pool.release(gb, "act")
                            b.w("scalar", "pe", ub["done"])
                            b.do("scalar", lambda e, t=ub.get("tf", ub["t"]), th=th, oc=oc: e.activation(
                                ubuf[th][:, :], t[:, :], AF.Copy, scale=s_su[:, oc:oc + 1]),
                                [("act", 1)])
                            pool.release(ub, "act")
                            b.w("vector", "act", b.c("act"))
                            b.w("vector", "pe", st["down_pe"].get(og - 2))
                            r = b.do("vector", lambda e, th=th, fsl=fsl: e.tensor_mul(
                                fuse[:, fsl * T + th * 512: fsl * T + th * 512 + 512],
                                gsl[th][:, :], ubuf[th][:, :]), [("dve", 1)])
                            fuse_dve = r["dve"]
                        if ol == 7:
                            fuse_done[og] = b.c("dve")
                    if og >= 1:
                        dog = og - 1
                        pi = dog * 8 + ol
                        ds = pi % 2
                        ffn_wd_cast(b, st, pi)
                        if wd_next < 64:
                            ffn_wd_dma(b, st, wd_next)
                            wd_next += 1
                        b.w("tensor", "dve", st["wd_cast"][pi])
                        b.w("tensor", "dve", fuse_done[dog])
                        for dcl in range(2):
                            dc = 2 * ol + dcl
                            for th in range(NTH):
                                bk = pool6.acquire()
                                for o2 in range(8):
                                    b.do("tensor", lambda e, t=bk.get("tf", bk["t"]), ds=ds, dcl=dcl,
                                         o2=o2, th=th, dog=dog, st_=(o2 == 0),
                                         sp=(o2 == 7): e.matmul(
                                             t[:, :],
                                             wdb[ds][:, (dcl * 8 + o2) * 128:
                                                     (dcl * 8 + o2) * 128 + 128],
                                             fuse[:, ((dog % 2) * 8 + o2) * T + th * 512:
                                                  ((dog % 2) * 8 + o2) * T + th * 512 + 512],
                                             start=st_, stop=sp),
                                         [("pe", 1)] if o2 == 7 else [])
                                b.w("scalar", "pe", b.c("pe"))
                                b.w("scalar", "dve", datmp_hist.get(th))
                                b.do("scalar", lambda e, t=bk.get("tf", bk["t"]), dc=dc, th=th:
                                     e.activation(datmp[th][:, :], t[:, :], AF.Copy,
                                                  scale=s_sdh[:, dc:dc + 1]), [("act", 1)])
                                pool.release(bk, "act")
                                b.w("vector", "act", b.c("act"))
                                r = b.do("vector", lambda e, dc=dc, th=th: e.tensor_add(
                                    xs(dc, th), datmp[th][:, :], xs(dc, th)), [("dve", 1)])
                                datmp_hist[th] = r["dve"]
                                st["down_evac"][(dc, th)] = r["dve"]
                        slot_mm[("d", ds)] = b.c("pe")
                        st["down_pe"][dog] = b.c("pe")

        def emit_proj(b, wsrc, dst, act_fn, st32, stb, transpose_to=None,
                      act_scale=None, pl=None):
            pl = pl or pool
            """dst[oc-part, t] = act(rn @ W^T). If transpose_to is given, the
            activated [e,t] halves are routed through og_st and PE-transposed
            into transpose_to ([t-part, e] blocks at tc*D + oc*128)."""
            b.w("sync", "act", b.cnt["act"])
            b.w("scalar", "pe", b.cnt["pe"])
            cast_done = {}
            ps_dmad = {}
            lastmm = {}

            def ps_dma(oc):
                if oc > 1:
                    b.w("sync", "act", cast_done[oc - 2])
                sem = "s_ps" if oc % 2 == 0 else "s_ps1"
                b.do("sync", lambda e, oc=oc, st=st32[oc % 2]: e.dma_start(
                    out=st[:, :], in_=wsrc.ap()[oc, :, :, :].rearrange("p dc w -> p (dc w)")),
                    [(sem, 16)])
                ps_dmad[oc] = (sem, b.c(sem))

            def ps_cast(oc):
                slot = oc % 2
                b.w("scalar", ps_dmad[oc][0], ps_dmad[oc][1])
                b.w("scalar", "pe", lastmm.get(slot))
                b.do("scalar", lambda e, s=slot, st=st32[oc % 2], stb=stb: e.copy(
                    stb[s][:, :], st[:, :]), [("act", 1)])
                cast_done[oc] = b.c("act")

            pend = []
            tbi = [0]

            def flush_tr(unit):
                oc_, th_, osl_, actc = unit
                tb = ptb[tbi[0] % 2]
                tbi[0] += 1
                b.w("tensor", "act", actc)
                tb_wait(tb)
                for s in range(4):
                    b.do("tensor", lambda e, t=tb["t"], s=s, osl=osl_: e.transpose(
                        t[:, s * 128:(s + 1) * 128],
                        og_st[osl_][:, s * 128:(s + 1) * 128], s_id[:, :]),
                        [("pe", 1)] if s == 3 else [])
                lastmm[("og", osl_)] = b.c("pe")
                b.w("vector", "pe", b.c("pe"))
                for s in range(4):
                    tc = th_ * 4 + s
                    b.do("vector", lambda e, t=tb["t"], s=s, tc=tc, oc=oc_:
                         e.tensor_copy(
                             transpose_to[:, tc * D + oc_ * 128: tc * D + oc_ * 128 + 128],
                             t[:, s * 128:(s + 1) * 128]),
                         [("dve", 1)])
                tb_release(tb, "dve")

            ps_dma(0)
            ps_dma(1)
            ps_cast(0)
            for oc in range(16):
                slot = oc % 2
                if oc + 2 < 16:
                    ps_dma(oc + 2)
                if oc + 1 < 16:
                    ps_cast(oc + 1)
                b.w("tensor", "act", cast_done[oc])
                for th in range(NTH):
                    bk = pl.acquire()
                    for dc in range(NDC):
                        b.do("tensor", lambda e, t=bk.get("tf", bk["t"]), s=slot, stb=stb,
                             dc=dc, th=th, st=(dc == 0), sp=(dc == NDC - 1): e.matmul(
                                 t[:, :], stb[s][:, dc * 128:(dc + 1) * 128],
                                 xn[:, dc * T + th * 512: dc * T + th * 512 + 512],
                                 start=st, stop=sp),
                             [("pe", 1)] if dc == NDC - 1 else [])
                    if transpose_to is None:
                        b.w("scalar", "pe", b.c("pe"))
                        b.do("scalar", lambda e, t=bk.get("tf", bk["t"]), dst=dst, oc=oc,
                             th=th: e.activation(
                                 dst[:, oc * T + th * 512: oc * T + th * 512 + 512],
                                 t[:, :], act_fn), [("act", 1)])
                        pl.release(bk, "act")
                    else:
                        mm_done = b.c("pe")
                        if pend:
                            flush_tr(pend.pop(0))
                        osl = th % 2
                        b.w("scalar", "pe", mm_done)
                        b.w("scalar", "pe", lastmm.get(("og", osl)))
                        if act_scale is None:
                            b.do("scalar", lambda e, t=bk["t"], osl=osl: e.activation(
                                og_st[osl][:, :], t[:, :], act_fn), [("act", 1)])
                        else:
                            b.do("scalar", lambda e, t=bk["t"], osl=osl: e.activation(
                                og_st[osl][:, :], t[:, :], act_fn,
                                scale=act_scale[:, :1]), [("act", 1)])
                        pool.release(bk, "act")
                        pend.append((oc, th, osl, b.c("act")))
                lastmm[slot] = b.c("pe")
            while pend:
                flush_tr(pend.pop(0))

        def emit_attn(b, upto=9, x_ready=None):
            for lo, hi in ((0, 8), (8, 16)):
                if x_ready:
                    sems = {s for (dc, th), (s, v) in x_ready.items() if lo <= dc < hi}
                    for sname in sorted(sems):
                        b.w("scalar", sname,
                            max(v for (dc, th), (s, v) in x_ready.items()
                                if s == sname and lo <= dc < hi))
                b.do("scalar", lambda e, lo=lo, hi=hi: e.dma_start(
                    out=park.ap().rearrange("(dc p) t -> p dc t", p=128)[:, lo:hi, :],
                    in_=x.ap().rearrange("p (dc t) -> p dc t", t=T)[:, lo:hi, :]),
                    [("s_park", 16)])
            park_done = b.c("s_park")
            emit_ln(b, kind=2, x_ready=x_ready)
            b.barrier()
            if upto <= 1:
                return
            emit_proj(b, wkT, kbuf, AF.Silu, pstage32, pstageb, pl=pool6)
            emit_proj(b, wvT, vbuf, AF.Silu, pstage32, pstageb, pl=pool6)
            b.barrier()
            if upto <= 2:
                return
            b.w("vector", "s_park", park_done)
            pairs = [("v", ec, jc) for jc in range(NJC) for ec in range(NDC)]
            pairs += [("k", dc, jc) for jc in range(NJC) for dc in range(NDC)]
            for bi in range(0, len(pairs), 4):
                tb = ptb[(bi // 4) % 2]
                tb_wait(tb)
                for s in range(4):
                    kind, cc_, jc = pairs[bi + s]
                    srcb = vbuf if kind == "v" else kbuf
                    b.do("tensor", lambda e, t=tb["t"], s=s, srcb=srcb, cc_=cc_, jc=jc:
                         e.transpose(t[:, s * 128:(s + 1) * 128],
                                     srcb[:, cc_ * T + jc * 128: cc_ * T + jc * 128 + 128],
                                     s_id[:, :]),
                         [("pe", 1)] if s == 3 else [])
                b.w("vector", "pe", b.c("pe"))
                for s in range(4):
                    kind, cc_, jc = pairs[bi + s]
                    dstt = v_tm if kind == "v" else k_tm
                    b.do("vector", lambda e, t=tb["t"], s=s, dstt=dstt, cc_=cc_, jc=jc:
                         e.tensor_copy(dstt[:, jc * D + cc_ * 128: jc * D + cc_ * 128 + 128],
                                       t[:, s * 128:(s + 1) * 128]),
                         [("dve", 1)])
                tb_release(tb, "dve")
            b.barrier()
            if upto <= 3:
                return
            for er in range(NER):
                b.w("scalar", "s_bo", b.c("s_bo"))
                for dc in range(NDC):
                    bk = pool6.acquire()
                    for jc in range(NJC):
                        b.do("tensor", lambda e, t=bk.get("tf", bk["t"]), jc=jc, dc=dc, er=er,
                             st=(jc == 0), sp=(jc == NJC - 1): e.matmul(
                                 t[:, :], k_tm[:, jc * D + dc * 128: jc * D + dc * 128 + 128],
                                 v_tm[:, jc * D + er * 512: jc * D + er * 512 + 512],
                                 start=st, stop=sp),
                             [("pe", 1)] if jc == NJC - 1 else [])
                    b.w("scalar", "pe", b.c("pe"))
                    b.do("scalar", lambda e, t=bk.get("tf", bk["t"]), dc=dc: e.activation(
                        bstage[:, dc * 512:(dc + 1) * 512], t[:, :], AF.Copy,
                        scale=s_bf[:, :1]), [("act", 1)])
                    pool6.release(bk, "act")
                b.w("scalar", "act", b.c("act"))
                for half in range(2):
                    b.do("scalar", lambda e, er=er, half=half: e.dma_start(
                        out=kvc.ap().rearrange("(dc p) e -> p dc e", p=128)
                            [:, half * 8:(half + 1) * 8, er * 512:(er + 1) * 512],
                        in_=bstage.ap().rearrange("p (dc e) -> p dc e", e=512)
                            [:, half * 8:(half + 1) * 8, :]), [("s_bo", 16)])
            b.w("gpsimd", "s_bo", b.c("s_bo"))
            if USE_CC:
                b.do("gpsimd", lambda e: e.collective_compute(
                    "AllReduce", ALU.add,
                    replica_groups=[[0, 1], [2, 3], [4, 5], [6, 7]],
                    ins=[kvc.ap().opt()], outs=[kvs.ap().opt()]), [("cc", 1)])
            else:
                b.do("gpsimd", lambda e: e.dma_start(out=kvs.ap(), in_=kvc.ap()),
                     [("cc", 16)])
            b.barrier()
            if upto <= 4:
                return
            b.w("sync", "s_bo", b.c("s_bo"))
            emit_proj(b, wqT, qbuf, AF.Silu, qg32, qgb, pl=pool6)
            b.barrier()
            if upto <= 5:
                return
            emit_proj(b, wgT, None, AF.Tanh, qg32, qgb, transpose_to=g_tm,
                      act_scale=s_half)
            b.barrier()
            if upto <= 6:
                return
            for tr in range(NTH):
                for jc in range(0, (tr + 1) * 4):
                    bk = pool6.acquire()
                    for dc in range(NDC):
                        b.do("tensor", lambda e, t=bk.get("tf", bk["t"]), jc=jc, tr=tr, dc=dc,
                             st=(dc == 0), sp=(dc == NDC - 1): e.matmul(
                                 t[:, :], kbuf[:, dc * T + jc * 128: dc * T + jc * 128 + 128],
                                 qbuf[:, dc * T + tr * 512: dc * T + tr * 512 + 512],
                                 start=st, stop=sp),
                             [("pe", 1)] if dc == NDC - 1 else [])
                    pe_g = b.c("pe")
                    lo_tc = tr * 4
                    if jc < lo_tc:
                        b.w("scalar", "pe", pe_g)
                        b.do("scalar", lambda e, t=bk.get("tf", bk["t"]), jc=jc, tr=tr:
                             e.activation(
                                 s_t[:, jc * T + tr * 512: jc * T + tr * 512 + 512], t[:, :],
                                 AF.Copy), [("act", 1)])
                        pool6.release(bk, "act")
                    else:
                        s = jc - lo_tc
                        b.w("vector", "pe", pe_g)
                        b.do("vector", lambda e, t=bk.get("tf", bk["t"]), jc=jc, tr=tr, s=s: e.tensor_mul(
                            s_t[:, jc * T + tr * 512 + s * 128:
                                jc * T + tr * 512 + s * 128 + 128],
                            t[:, s * 128:(s + 1) * 128], s_mask[:, :]),
                            [("dve", 1)])
                        if s < 3:
                            b.do("vector", lambda e, t=bk.get("tf", bk["t"]), jc=jc, tr=tr,
                                 s=s: e.tensor_copy(
                                     s_t[:, jc * T + tr * 512 + (s + 1) * 128:
                                         jc * T + (tr + 1) * 512],
                                     t[:, (s + 1) * 128:512]),
                                 [("dve", 1)])
                        pool6.release(bk, "dve")
            b.barrier()
            if upto <= 7:
                return
            zflag = {}
            apply_mm_done = {}

            def z_load(er):
                zs = er % 2
                if er >= 2:
                    b.w("scalar", "pe", apply_mm_done[er - 2])
                b.w("scalar", "cc", 1)
                for half in range(2):
                    b.do("scalar", lambda e, er=er, zs=zs, half=half: e.dma_start(
                        out=zbuf[zs].ap().rearrange("p (dc e) -> p dc e", e=512)
                            [:, half * 8:(half + 1) * 8, :],
                        in_=kvs.ap().rearrange("(dc p) e -> p dc e", p=128)
                            [:, half * 8:(half + 1) * 8, er * 512:(er + 1) * 512]),
                        [(f"s_z{zs}", 16)])
                b.w("vector", f"s_z{zs}", b.c(f"s_z{zs}"))
                if er >= 2:
                    b.w("vector", "pe", apply_mm_done[er - 2])
                b.do("vector", lambda e, zs=zs: e.tensor_scalar_mul(
                    zbuf[zs][:, :], zbuf[zs][:, :], s_af[:, :1]), [("dve", 1)])
                zflag[er] = b.c("dve")

            z_load(0)
            z_load(1)
            og_pend = []
            og_hist = {}
            og_tbi = [0]

            def og_flush(unit):
                er_, tc_, sl_, dvec = unit
                tb = ptb[og_tbi[0] % 2]
                og_tbi[0] += 1
                b.w("tensor", "dve", dvec)
                tb_wait(tb)
                for s in range(4):
                    b.do("tensor", lambda e, t=tb["t"], s=s, sl=sl_: e.transpose(
                        t[:, s * 128:(s + 1) * 128], og_st[sl_][:, s * 128:(s + 1) * 128],
                        s_id[:, :]), [("pe", 1)] if s == 3 else [])
                og_hist[sl_] = b.c("pe")
                b.w("vector", "pe", b.c("pe"))
                for s in range(4):
                    ec = er_ * 4 + s
                    b.do("vector", lambda e, t=tb["t"], s=s, ec=ec, tc=tc_:
                         e.tensor_copy(ogfm[:, ec * T + tc_ * 128: ec * T + tc_ * 128 + 128],
                                       t[:, s * 128:(s + 1) * 128]),
                         [("dve", 1)])
                tb_release(tb, "dve")

            for er in range(NER):
                zs = er % 2
                b.w("tensor", "dve", zflag[er])
                for tc in range(NTC):
                    bk = pool.acquire()
                    for jc in range(tc + 1):
                        b.do("tensor", lambda e, t=bk["t"], jc=jc, tc=tc, er=er,
                             st=(jc == 0): e.matmul(
                                 t[:, :], s_t[:, jc * T + tc * 128: jc * T + tc * 128 + 128],
                                 v_tm[:, jc * D + er * 512: jc * D + er * 512 + 512],
                                 start=st, stop=False), [])
                    for dc in range(NDC):
                        b.do("tensor", lambda e, t=bk["t"], dc=dc, tc=tc, zs=zs,
                             sp=(dc == NDC - 1): e.matmul(
                                 t[:, :], qbuf[:, dc * T + tc * 128: dc * T + tc * 128 + 128],
                                 zbuf[zs][:, dc * 512:(dc + 1) * 512],
                                 start=False, stop=sp),
                             [("pe", 1)] if dc == NDC - 1 else [])
                    mm_done = b.c("pe")
                    if og_pend:
                        og_flush(og_pend.pop(0))
                    sl = tc % 2
                    b.w("vector", "pe", mm_done)
                    b.w("vector", "pe", og_hist.get(sl))
                    b.do("vector", lambda e, t=bk["t"], tc=tc, er=er, sl=sl:
                         e.scalar_tensor_tensor(
                             og_st[sl][:, :],
                             g_tm[:, tc * D + er * 512: tc * D + er * 512 + 512],
                             1.0, t[:, :], ALU.add, ALU.mult),
                         [("dve", 1)])
                    pool.release(bk, "dve")
                    og_pend.append((er, tc, sl, b.c("dve")))
                apply_mm_done[er] = b.c("pe")
                if er + 2 < NER:
                    z_load(er + 2)
            while og_pend:
                og_flush(og_pend.pop(0))
            b.barrier()
            if upto <= 8:
                return
            mixer_pe = b.c("pe")
            mixer_dve = b.c("dve")
            b.w("scalar", "pe", mixer_pe)
            b.w("scalar", "dve", mixer_dve)
            for dc in range(NDC):
                sx = "s_xla" if dc < 8 else "s_xlb"
                b.do("scalar", lambda e, dc=dc: e.dma_start(
                    out=x.ap().rearrange("p (dc t) -> p dc t", t=T)[:, dc:dc + 1, :],
                    in_=park.ap().rearrange("(dc p) t -> p dc t", p=128)[:, dc:dc + 1, :]),
                    [(sx, 16)])
            xla_done = b.c("s_xla")
            xlb_done = b.c("s_xlb")
            wp_cast = {}
            wp_dmad = {}
            wp_mm = {}
            wp_evac = {}

            def wp_dma(dc):
                if dc > 1:
                    b.w("sync", "act", wp_cast[dc - 2])
                sem = "s_ps" if dc % 2 == 0 else "s_ps1"
                b.do("sync", lambda e, dc=dc, st=wp32[dc % 2]: e.dma_start(
                    out=st[:, :], in_=wpT.ap()[dc, :, :, :].rearrange("p ec w -> p (ec w)")),
                    [(sem, 16)])
                wp_dmad[dc] = (sem, b.c(sem))

            def wp_castf(dc):
                sl = dc % 2
                b.w("scalar", wp_dmad[dc][0], wp_dmad[dc][1])
                b.w("scalar", "pe", wp_mm.get(sl))
                b.do("scalar", lambda e, sl=sl, st=wp32[dc % 2]: e.activation(
                    wpb[sl][:, :], st[:, :], AF.Copy, scale=s_half[:, :1]),
                    [("act", 1)])
                wp_cast[dc] = b.c("act")

            wp_dma(0)
            wp_dma(1)
            wp_castf(0)
            for dc in range(NDC):
                sl = dc % 2
                if dc + 2 < NDC:
                    wp_dma(dc + 2)
                if dc + 1 < NDC:
                    wp_castf(dc + 1)
                b.w("tensor", "act", wp_cast[dc])
                for th in range(NTH):
                    bk = pool6.acquire()
                    for ec in range(NDC):
                        b.do("tensor", lambda e, t=bk.get("tf", bk["t"]), sl=sl, ec=ec, th=th,
                             st=(ec == 0), sp=(ec == NDC - 1): e.matmul(
                                 t[:, :], wpb[sl][:, ec * 128:(ec + 1) * 128],
                                 ogfm[:, ec * T + th * 512: ec * T + th * 512 + 512],
                                 start=st, stop=sp),
                             [("pe", 1)] if ec == NDC - 1 else [])
                    b.w("vector", "pe", b.c("pe"))
                    b.w("vector", "s_xla" if dc < 8 else "s_xlb",
                        xla_done if dc < 8 else xlb_done)
                    r = b.do("vector", lambda e, t=bk.get("tf", bk["t"]), dc=dc, th=th:
                             e.tensor_add(xs(dc, th), t[:, :], xs(dc, th)), [("dve", 1)])
                    wp_evac[(dc, th)] = ("dve", r["dve"])
                    pool6.release(bk, "dve")
                wp_mm[sl] = b.c("pe")
            return wp_evac

        def program(b):
            b.do("sync", lambda e: e.dma_start(
                out=x.ap().rearrange("p (dc t) -> p dc t", t=T)[:, 0:8, :],
                in_=xT.ap().rearrange("(dc p) t -> p dc t", p=128)[:, 0:8, :]), [("s_x", 16)])
            b.do("sync", lambda e: e.dma_start(
                out=x.ap().rearrange("p (dc t) -> p dc t", t=T)[:, 8:16, :],
                in_=xT.ap().rearrange("(dc p) t -> p dc t", p=128)[:, 8:16, :]), [("s_x2", 16)])
            b.do("sync", lambda e: e.dma_start(out=s_sg[:, :], in_=sgate.ap()), [("s_pre", 16)])
            b.do("sync", lambda e: e.dma_start(out=s_su[:, :], in_=sup.ap()), [("s_pre", 16)])
            b.do("sync", lambda e: e.dma_start(out=s_sdh[:, :], in_=sdown.ap()), [("s_pre", 16)])
            b.do("sync", lambda e: e.dma_start(out=s_id[:, :], in_=ident_in.ap()), [("s_pre", 16)])
            b.do("sync", lambda e: e.dma_start(out=s_mask[:, :], in_=mask_in.ap()), [("s_pre", 16)])
            b.do("sync", lambda e: e.dma_start(out=s_bf[:, :], in_=bflag_in.ap()), [("s_pre", 16)])
            b.do("sync", lambda e: e.dma_start(out=s_af[:, :], in_=aflag_in.ap()), [("s_pre", 16)])
            for eng in ("vector", "scalar", "tensor", "gpsimd"):
                b.w(eng, "s_pre", b.c("s_pre"))
            b.do("vector", lambda e: e.memset(ones_c[:, :], 1.0), [("dve", 1)])
            b.do("vector", lambda e: e.memset(ones_r[:, :], 1.0), [("dve", 1)])
            b.do("vector", lambda e: e.memset(c_e1b[:, :], EPS_LN), [("dve", 1)])
            b.do("vector", lambda e: e.memset(c_e1s[:, :], 1.0), [("dve", 1)])
            b.do("vector", lambda e: e.memset(c_e2b[:, :], EPS_F32 * EPS_LN), [("dve", 1)])
            b.do("vector", lambda e: e.memset(c_e2s[:, :], 1.0 + EPS_F32), [("dve", 1)])
            b.do("vector", lambda e: e.tensor_scalar_mul(s_sdh[:, :], s_sdh[:, :], 0.5),
                 [("dve", 1)])
            b.do("vector", lambda e: e.memset(s_half[:, :], 0.5), [("dve", 1)])
            b.do("vector", lambda e: e.memset(ones_cb[:, :], 1.0), [("dve", 1)])
            b.barrier()

            xr = {(dc, th): ("s_x" if dc < 8 else "s_x2", 16)
                  for dc in range(NDC) for th in range(NTH)}
            if "ffn1" in stages:
                st1 = ffn_prolog(b)
                ad = emit_ln(b, kind=1, x_ready=xr)
                emit_ffn(b, st1, ad)
                xr = {k: ("dve", v) for k, v in st1["down_evac"].items()}
            if "attn" in stages:
                xr = emit_attn(b, upto=ATTN_UPTO, x_ready=xr)
            if "ffn2" in stages:
                st2 = ffn_prolog(b)
                ad = emit_ln(b, kind=1, x_ready=xr)
                emit_ffn(b, st2, ad)
                xr = {k: ("dve", v) for k, v in st2["down_evac"].items()}

            h0 = max(v for (dc, th), (s, v) in xr.items() if dc < 8) if xr else None
            h1 = max(v for (dc, th), (s, v) in xr.items() if dc >= 8) if xr else None
            b.w("gpsimd", "dve", h0)
            b.do("gpsimd", lambda e: e.dma_start(
                out=xout.ap().rearrange("(dc p) t -> p dc t", p=128)[:, 0:8, :],
                in_=x.ap().rearrange("p (dc t) -> p dc t", t=T)[:, 0:8, :]), [("st", 16)])
            b.w("gpsimd", "dve", h1)
            b.do("gpsimd", lambda e: e.dma_start(
                out=xout.ap().rearrange("(dc p) t -> p dc t", p=128)[:, 8:16, :],
                in_=x.ap().rearrange("p (dc t) -> p dc t", t=T)[:, 8:16, :]), [("st", 16)])
            b.w("gpsimd", "st", b.c("st"))

        def emit(engname, eng):
            b.begin_pass(engname, eng)
            pool.reset()
            pool6.idx = 0
            for tb_ in ptb:
                tb_["rel"] = []
            prow["R0"] = []
            prow["R1"] = []
            sq_mm_hist.clear()
            xb_hist.clear()
            atmp_hist.clear()
            program(b)

        blk.sync(lambda e: emit("sync", e))
        blk.tensor(lambda e: emit("tensor", e))
        blk.vector(lambda e: emit("vector", e))
        blk.scalar(lambda e: emit("scalar", e))
        blk.gpsimd(lambda e: emit("gpsimd", e))

    return nc


_CACHE = {}


def _get_nc(stages):
    key = tuple(stages)
    if key not in _CACHE:
        _CACHE[key] = build_program(stages)
    return _CACHE[key]


def _prep_host(L0_attn_x, L0_attn_Wq, L0_attn_Wk, L0_attn_Wv, L0_attn_Wg, L0_attn_Wp,
               L0_ffn_w_gate, L0_ffn_w_up, L0_ffn_w_down, L0_ffn_s_gate, L0_ffn_s_up,
               L0_ffn_s_down):
    def lhsT_tiles(w):
        return np.ascontiguousarray(
            np.asarray(w, np.float32).reshape(16, 128, 16, 128).transpose(0, 3, 2, 1))

    shared = {
        "wqT": lhsT_tiles(L0_attn_Wq),
        "wkT": lhsT_tiles(L0_attn_Wk),
        "wvT": lhsT_tiles(L0_attn_Wv),
        "wgT": lhsT_tiles(L0_attn_Wg),
        "wpT": lhsT_tiles(L0_attn_Wp),
        "wgate": np.ascontiguousarray(
            np.asarray(L0_ffn_w_gate).reshape(16, 128, 64, 128).transpose(2, 1, 0, 3)),
        "wup": np.ascontiguousarray(
            np.asarray(L0_ffn_w_up).reshape(16, 128, 64, 128).transpose(2, 1, 0, 3)),
        "wdown": np.ascontiguousarray(
            np.asarray(L0_ffn_w_down).reshape(8, 8, 128, 16, 128).transpose(0, 3, 2, 1, 4)),
        "sgate": np.ascontiguousarray(np.asarray(L0_ffn_s_gate, np.float32).reshape(64, 128).T),
        "sup": np.ascontiguousarray(np.asarray(L0_ffn_s_up, np.float32).reshape(64, 128).T),
        "sdown": np.ascontiguousarray(np.asarray(L0_ffn_s_down, np.float32).reshape(16, 128).T),
        "ident": np.eye(128, dtype=ml_dtypes.bfloat16),
        "mask": np.ascontiguousarray(np.triu(np.ones((128, 128), np.float32))),
    }
    xf = np.asarray(L0_attn_x, np.float32)
    in_maps = []
    for c in range(8):
        bb, hh = c // 2, c % 2
        m = dict(shared)
        m["xT"] = np.ascontiguousarray(xf[bb, hh * T:(hh + 1) * T, :].T)
        even = 1.0 if hh == 0 else 0.0
        m["bflag"] = np.full((128, 1), even, np.float32)
        m["aflag"] = np.full((128, 1), 1.0 - even, np.float32)
        in_maps.append(m)
    return in_maps


def kernel(**inputs):
    in_maps = _prep_host(**inputs)
    nc = _get_nc(STAGES)
    res = run_bass_kernel_spmd(nc, in_maps, core_ids=list(range(8)))
    out = np.empty((4, 2048, 2048), np.float32)
    for c in range(8):
        bb, hh = c // 2, c % 2
        out[bb, hh * T:(hh + 1) * T, :] = res.results[c]["xout"].T
    return out


# version 14
# speedup vs baseline: 1.1877x; 1.1877x over previous
import numpy as np
import ml_dtypes
import concourse.bass as bass
import concourse.mybir as mybir
from concourse.bass_utils import run_bass_kernel_spmd

F32 = mybir.dt.float32
BF16 = mybir.dt.bfloat16
I8 = mybir.dt.int8
AF = mybir.ActivationFunctionType
ALU = mybir.AluOpType

T = 1024
D = 2048
NDC = 16
NOG = 4
NTH = 2
NTC = 8
NJC = 8
NER = 4

EPS_LN = 1e-5
EPS_F32 = float(np.finfo(np.float32).eps)

STAGES = ("ffn1", "attn", "ffn2")
USE_CC = True
ATTN_UPTO = 9

BASE = 16512
R1 = BASE
R2 = BASE + 65536
R3 = BASE + 98304
R4 = BASE + 131072
R5 = BASE + 163840
H = BASE + 196608
SBUF_TOP = 229344


class B:

    def __init__(self, nc):
        self.nc = nc
        self.cur = None
        self.e = None
        self.sems = {}
        self.cnt = {}

    def sem_init(self, names, es):
        for n in names:
            self.sems[n] = es.enter_context(self.nc.semaphore(n))

    def begin_pass(self, eng, handle):
        self.cur = eng
        self.e = handle
        for n in self.sems:
            self.cnt[n] = 0
        self.hwm = {}

    def w(self, eng, sname, val):
        if val is None or val <= 0:
            return
        key = (eng, sname)
        if self.hwm.get(key, -1) >= val:
            return
        self.hwm[key] = val
        if self.cur == eng:
            self.e.wait_ge(self.sems[sname], val)

    def do(self, eng, fn, incs=()):
        out = {}
        if self.cur == eng:
            ins = fn(self.e)
            for s, n in incs:
                ins = ins.then_inc(self.sems[s], n)
        for s, n in incs:
            self.cnt[s] += n
            out[s] = self.cnt[s]
        return out

    def c(self, sname):
        return self.cnt[sname]

    def barrier(self):
        snap = {s: self.cnt[s] for s in ("pe", "dve", "act", "pool")}
        m = {"tensor": "pe", "vector": "dve", "scalar": "act", "gpsimd": "pool"}
        for eng, own in m.items():
            for o_sem in snap:
                if o_sem != own:
                    self.w(eng, o_sem, snap[o_sem])
        for o_sem in snap:
            self.w("sync", o_sem, snap[o_sem])


class Pool:

    def __init__(self, b, banks):
        self.b = b
        self.banks = [{"t": t, "rel": []} for t in banks]
        self.idx = 0

    def reset(self):
        self.idx = 0
        for bk in self.banks:
            bk["rel"] = []

    def acquire(self):
        bk = self.banks[self.idx % len(self.banks)]
        self.idx += 1
        for sem, cntv in bk["rel"]:
            self.b.w("tensor", sem, cntv)
        return bk

    def release(self, bk, *sems):
        bk["rel"] = [(s, self.b.cnt[s]) for s in sems]


def build_program(stages, use_cc=None):
    global USE_CC
    if use_cc is not None:
        USE_CC = use_cc
    nc = bass.Bass("TRN2", target_bir_lowering=False, debug=False)

    xT = nc.dram_tensor("xT", [D, T], F32, kind="ExternalInput")
    wqT = nc.dram_tensor("wqT", [16, 128, 16, 128], F32, kind="ExternalInput")
    wkT = nc.dram_tensor("wkT", [16, 128, 16, 128], F32, kind="ExternalInput")
    wvT = nc.dram_tensor("wvT", [16, 128, 16, 128], F32, kind="ExternalInput")
    wgT = nc.dram_tensor("wgT", [16, 128, 16, 128], F32, kind="ExternalInput")
    wpT = nc.dram_tensor("wpT", [16, 128, 16, 128], F32, kind="ExternalInput")
    wgate = nc.dram_tensor("wgate", [64, 128, 16, 128], I8, kind="ExternalInput")
    wup = nc.dram_tensor("wup", [64, 128, 16, 128], I8, kind="ExternalInput")
    wdown = nc.dram_tensor("wdown", [8, 16, 128, 8, 128], I8, kind="ExternalInput")
    sgate = nc.dram_tensor("sgate", [128, 64], F32, kind="ExternalInput")
    sup = nc.dram_tensor("sup", [128, 64], F32, kind="ExternalInput")
    sdown = nc.dram_tensor("sdown", [128, 16], F32, kind="ExternalInput")
    ident_in = nc.dram_tensor("ident", [128, 128], BF16, kind="ExternalInput")
    mask_in = nc.dram_tensor("mask", [128, 128], F32, kind="ExternalInput")
    bflag_in = nc.dram_tensor("bflag", [128, 1], F32, kind="ExternalInput")
    aflag_in = nc.dram_tensor("aflag", [128, 1], F32, kind="ExternalInput")
    xout = nc.dram_tensor("xout", [D, T], F32, kind="ExternalOutput")

    park = nc.dram_tensor("park", [D, T], F32)
    kvc = nc.dram_tensor("kvc", [D, D], BF16)
    kvs = nc.dram_tensor("kvs", [D, D], BF16)

    A = lambda name, shape, dt, off: nc.alloc_sbuf_tensor_at(name, shape, dt, offset=off)
    x = A("x", [128, NDC * T], F32, R1)
    pstage32 = [A("pstage32a", [128, 2048], F32, R5),
                A("pstage32b", [128, 2048], F32, R5 + 16384)]
    pstageb = [A(f"pstageb{i}", [128, 2048], BF16, R5 + 8192 + i * 4096) for i in range(2)]
    v_tm = A("v_tm", [128, NJC * D], BF16, R1)
    g_tm = A("g_tm", [128, NTC * D], BF16, R1 + 32768)
    xn = A("xn", [128, NDC * T], BF16, R2)
    s_t = A("s_t", [128, NJC * T], BF16, R2)
    og_st = [A(f"og_st{i}", [128, 512], BF16, H + 12320 + i * 1024) for i in range(2)]
    xbuf = og_st + [A("xbuf2", [128, 512], BF16, H + 14432)]
    assert H + 15456 <= SBUF_TOP
    s_half = A("s_half", [128, 1], F32, H + 14368)
    ones_cb = A("ones_cb", [128, 1], BF16, H + 14400)
    wp32 = [A("wp32a", [128, 2048], F32, R2),
            A("wp32b", [128, 2048], F32, R2 + 16384)]
    wpb = [A(f"wpb{i}", [128, 2048], BF16, R2 + 8192 + i * 4096) for i in range(2)]
    fuse = A("fuse", [128, 16 * T], BF16, R3)
    kbuf = A("kbuf", [128, NDC * T], BF16, R3)
    zbuf = [A(f"zbuf{i}", [128, 16 * 512], BF16, R3 + i * 16384) for i in range(2)]
    vbuf = A("vbuf", [128, NDC * T], BF16, R4)
    wd8 = [A(f"wd8{i}", [128, 2048], I8, R4 + i * 2048) for i in range(2)]
    wdb = [A(f"wdb{i}", [128, 2048], BF16, R4 + 4096 + i * 4096) for i in range(2)]
    gsl = [A(f"gsl{i}", [128, 512], BF16, R4 + 12288 + i * 1024) for i in range(2)]
    ubuf = [A(f"ubuf{i}", [128, 512], BF16, R4 + 14336 + i * 1024) for i in range(2)]
    lnsig = [A(f"lnsig{i}", [128, 512], F32, R4 + 4096 + i * 2048) for i in range(2)]
    atmp = [A(f"atmp{i}", [128, 512], F32, R4 + 12288 + i * 2048) for i in range(2)]
    bstage = A("bstage", [128, 16 * 512], BF16, R4)
    qg32 = [A("qg32a", [128, 2048], F32, R4 + 16384),
            A("qg32b", [128, 2048], F32, R4 + 8192)]
    datmp = [A(f"datmp{i}", [128, 512], F32, R4 + 16384 + i * 2048) for i in range(2)]
    qgb = [A(f"qgb{i}", [128, 2048], BF16, R4 + 24576 + i * 4096) for i in range(2)]
    ogfm = A("ogfm", [128, 16 * T], BF16, R4)
    wg8 = A("wg8", [128, 2048], I8, R5)
    wu8 = A("wu8", [128, 2048], I8, R5 + 2048)
    wgbf = [A(f"wgbf{i}", [128, 2048], BF16, R5 + 4096 + i * 4096) for i in range(2)]
    wubf = [A(f"wubf{i}", [128, 2048], BF16, R5 + 12288 + i * 4096) for i in range(2)]
    k_tm = A("k_tm", [128, NJC * D], BF16, R5)
    qbuf = A("qbuf", [128, NDC * T], BF16, R5)
    sq = [A(f"sq{i}", [128, 512], BF16, H + i * 1024) for i in range(4)]
    row_m = A("row_m", [1, 512], F32, H + 4096)
    row_s = A("row_s", [1, 512], F32, H + 6144)
    row_ms = A("row_ms", [1, 512], F32, H + 8192)
    s_sg = A("s_sg", [128, 64], F32, H + 10240)
    s_su = A("s_su", [128, 64], F32, H + 10496)
    s_sdh = A("s_sdh", [128, 16], F32, H + 10752)
    s_id = A("s_id", [128, 128], BF16, H + 10816)
    s_mask = A("s_mask", [128, 128], F32, H + 11072)
    s_bf = A("s_bf", [128, 1], F32, H + 11584)
    s_af = A("s_af", [128, 1], F32, H + 11616)
    ones_c = A("ones_c", [128, 1], F32, H + 11648)
    ones_r = A("ones_r", [1, 128], F32, H + 11680)
    c_e1b = A("c_e1b", [1, 1], F32, H + 12192)
    c_e1s = A("c_e1s", [1, 1], F32, H + 12224)
    c_e2b = A("c_e2b", [1, 1], F32, H + 12256)
    c_e2s = A("c_e2s", [1, 1], F32, H + 12288)
    assert H + 14432 <= SBUF_TOP

    from contextlib import ExitStack
    es = ExitStack()
    with es:
        pA = es.enter_context(nc.psum_tensor("pA", [128, 512], F32))
        pB = es.enter_context(nc.psum_tensor("pB", [128, 512], F32))
        pC = es.enter_context(nc.psum_tensor("pC", [128, 512], F32))
        pD = es.enter_context(nc.psum_tensor("pD", [128, 512], F32))
        pT0 = es.enter_context(nc.psum_tensor("pT0", [128, 1024], BF16))
        pT1 = es.enter_context(nc.psum_tensor("pT1", [128, 1024], BF16))
        pT0f = pT0.bitcast(F32)
        pT1f = pT1.bitcast(F32)
        pR0 = es.enter_context(nc.psum_tensor("pR0", [1, 512], F32))
        pR1 = es.enter_context(nc.psum_tensor("pR1", [1, 512], F32))

        b = B(nc)
        b.sem_init(["s_pre", "s_x", "s_x2", "s_wg", "s_wu", "s_wd0", "s_wd1", "s_ps", "s_ps1",
                    "s_park", "s_bo", "s_z0", "s_z1", "s_xla", "s_xlb",
                    "pe", "dve", "act", "pool", "cc", "st"], es)
        pool = Pool(b, [pA, pB, pC, pD])
        tbank0 = {"t": pT0, "tf": pT0f, "rel": []}
        tbank1 = {"t": pT1, "tf": pT1f, "rel": []}
        ptb = [tbank0, tbank1]
        pool6 = Pool(b, [])
        pool6.banks = pool.banks + [tbank0, tbank1]
        prow = {"R0": [], "R1": []}

        def tb_wait(tb_):
            for sem, cntv in tb_["rel"]:
                b.w("tensor", sem, cntv)

        def tb_release(tb_, *sems):
            tb_["rel"] = [(s, b.cnt[s]) for s in sems]

        blk = es.enter_context(nc.Block())

        def xs(dc, th):
            return x[:, dc * T + th * 512: dc * T + th * 512 + 512]

        sq_mm_hist = {}
        xb_hist = {}
        atmp_hist = {}

        def emit_ln(b, kind, x_ready=None):
            bias_sl = c_e2b if kind == 2 else c_e1b
            scale_sl = c_e2s if kind == 2 else c_e1s
            apply_done = {}
            tgt = {}
            for th in range(NTH):
                if th == 0:
                    for sem, cntv in prow["R0"] + prow["R1"]:
                        b.w("tensor", sem, cntv)
                    tgt[0] = [None, None, pR0, pR1, 0]
                else:
                    bk1 = pool.acquire()
                    bk2 = pool.acquire()
                    tgt[1] = [bk1, bk2, bk1["t"], bk2["t"], 0]
                for dc in range(NDC):
                    sl4 = (th * NDC + dc) % 3
                    if x_ready:
                        b.w("gpsimd", *x_ready[(dc, th)])
                        b.w("scalar", *x_ready[(dc, th)])
                    b.w("gpsimd", "pe", xb_hist.get(sl4))
                    rc = b.do("gpsimd", lambda e, dc=dc, th=th, sl4=sl4: e.tensor_copy(
                        xbuf[sl4][:, :], xs(dc, th)), [("pool", 1)])
                    b.w("gpsimd", "pe", sq_mm_hist.get(sl4))
                    rq = b.do("gpsimd", lambda e, dc=dc, th=th, sl4=sl4: e.tensor_mul(
                        sq[sl4][:, :], xs(dc, th), xs(dc, th)), [("pool", 1)])
                    b.w("tensor", "pool", rc["pool"])
                    r1 = b.do("tensor", lambda e, t=tgt[th][2], sl4=sl4, st=(dc == 0),
                              sp=(dc == NDC - 1): e.matmul(
                                  t[:1, :], ones_cb[:, :1], xbuf[sl4][:, :],
                                  start=st, stop=sp), [("pe", 1)])
                    xb_hist[sl4] = r1["pe"]
                    b.w("tensor", "pool", rq["pool"])
                    r2 = b.do("tensor", lambda e, t=tgt[th][3], sl4=sl4, st=(dc == 0),
                              sp=(dc == NDC - 1): e.matmul(
                                  t[:1, :], ones_cb[:, :1], sq[sl4][:, :],
                                  start=st, stop=sp), [("pe", 1)])
                    sq_mm_hist[sl4] = r2["pe"]
                tgt[th][4] = b.c("pe")
            for th in range(NTH):
                bk1, bk2, tS, tQ, stats_pe = tgt[th]
                b.w("vector", "pe", stats_pe)
                b.do("vector", lambda e, tS=tS: e.tensor_scalar_mul(
                    row_m[:, :], tS[:1, :], 1.0 / D), [("dve", 1)])
                if th == 0:
                    prow["R0"] = [("dve", b.cnt["dve"])]
                b.do("vector", lambda e, tQ=tQ: e.tensor_scalar_mul(
                    row_s[:, :], tQ[:1, :], 1.0 / D), [("dve", 1)])
                if th == 0:
                    prow["R1"] = [("dve", b.cnt["dve"])]
                else:
                    pool.release(bk1, "dve")
                    pool.release(bk2, "dve")
                b.w("vector", "dve", b.c("dve"))
                b.do("vector", lambda e: e.tensor_mul(row_ms[:, :], row_m[:, :], row_m[:, :]),
                     [("dve", 1)])
                b.w("vector", "dve", b.c("dve"))
                b.do("vector", lambda e: e.tensor_sub(row_s[:, :], row_s[:, :], row_ms[:, :]),
                     [("dve", 1)])
                b.w("scalar", "dve", b.c("dve"))
                b.do("scalar", lambda e: e.activation(
                    row_ms[:, :], row_s[:, :], AF.Sqrt, bias=bias_sl[:1, :1],
                    scale=scale_sl[:1, :1]), [("act", 1)])
                b.w("vector", "act", b.c("act"))
                b.do("vector", lambda e: e.reciprocal(row_s[:, :], row_ms[:, :]), [("dve", 1)])
                b.w("vector", "dve", b.c("dve"))
                b.do("vector", lambda e: e.tensor_mul(row_ms[:, :], row_m[:, :], row_s[:, :]),
                     [("dve", 1)])
                bkS = pool.acquire()
                b.w("tensor", "dve", b.c("dve"))
                b.do("tensor", lambda e, t=bkS["t"]: e.matmul(
                    t[:, :], ones_r[:1, :], row_s[:1, :], start=True, stop=True), [("pe", 1)])
                bkM = pool.acquire()
                b.do("tensor", lambda e, t=bkM["t"]: e.matmul(
                    t[:, :], ones_r[:1, :], row_ms[:1, :], start=True, stop=True), [("pe", 1)])
                b.w("scalar", "pe", b.c("pe"))
                b.do("scalar", lambda e, th=th, t=bkS["t"]: e.activation(
                    lnsig[th % 2][:, :], t[:, :], AF.Copy), [("act", 1)])
                pool.release(bkS, "act")
                sig_ready = b.c("act")
                b.w("gpsimd", "act", sig_ready)
                b.w("vector", "pe", b.c("pe"))
                for dc in range(NDC):
                    if x_ready:
                        b.w("gpsimd", *x_ready[(dc, th)])
                    b.w("gpsimd", "dve", atmp_hist.get(dc % 2))
                    r1 = b.do("gpsimd", lambda e, dc=dc, th=th: e.tensor_mul(
                        atmp[dc % 2][:, :], xs(dc, th), lnsig[th % 2][:, :]), [("pool", 1)])
                    b.w("vector", "pool", r1["pool"])
                    r2 = b.do("vector", lambda e, dc=dc, th=th, m=bkM["t"]:
                              e.tensor_sub(xn[:, dc * T + th * 512: dc * T + th * 512 + 512],
                                           atmp[dc % 2][:, :], m[:, :]), [("dve", 1)])
                    atmp_hist[dc % 2] = r2["dve"]
                apply_done[th] = b.c("dve")
                pool.release(bkM, "dve")
            return apply_done

        def ffn_gu_dma(b, st, oc):
            if oc > 0:
                b.w("sync", "dve", st["gu_cast"][oc - 1])
            b.do("sync", lambda e, oc=oc: e.dma_start(
                out=wg8[:, :],
                in_=wgate.ap()[oc, :, :, :].rearrange("p hc w -> p (hc w)")),
                [("s_wg", 16)])
            b.do("sync", lambda e, oc=oc: e.dma_start(
                out=wu8[:, :],
                in_=wup.ap()[oc, :, :, :].rearrange("p hc w -> p (hc w)")),
                [("s_wu", 16)])
            st["gu_dmad"][oc] = (b.c("s_wg"), b.c("s_wu"))

        def ffn_gu_cast(b, st, oc):
            slot = oc % 2
            b.w("vector", "s_wg", st["gu_dmad"][oc][0])
            b.w("vector", "pe", st["slot_mm"].get(("g", slot)))
            b.do("vector", lambda e, s=slot: e.tensor_copy(wgbf[s][:, :], wg8[:, :]),
                 [("dve", 1)])
            b.w("vector", "s_wu", st["gu_dmad"][oc][1])
            b.w("vector", "pe", st["slot_mm"].get(("u", slot)))
            b.do("vector", lambda e, s=slot: e.tensor_copy(wubf[s][:, :], wu8[:, :]),
                 [("dve", 1)])
            st["gu_cast"][oc] = b.c("dve")

        def ffn_wd_dma(b, st, pi):
            og, k = pi // 8, pi % 8
            ds = pi % 2
            if pi >= 2:
                b.w("sync", "dve", st["wd_cast"][pi - 2])
            b.do("sync", lambda e, og=og, k=k, ds=ds: e.dma_start(
                out=wd8[ds].ap().rearrange("p (dcl ol w) -> p dcl ol w", dcl=2, ol=8),
                in_=wdown.ap()[og, 2 * k: 2 * k + 2, :, :, :].rearrange(
                    "dcl p ol w -> p dcl ol w")), [(f"s_wd{ds}", 16)])
            st["wd_dmad"][pi] = b.c(f"s_wd{ds}")

        def ffn_wd_cast(b, st, pi):
            ds = pi % 2
            b.w("vector", f"s_wd{ds}", st["wd_dmad"][pi])
            b.w("vector", "pe", st["slot_mm"].get(("d", ds)))
            b.do("vector", lambda e, ds=ds: e.tensor_copy(wdb[ds][:, :], wd8[ds][:, :]),
                 [("dve", 1)])
            st["wd_cast"][pi] = b.c("dve")

        def ffn_prolog(b):
            st = {"gu_cast": {}, "gu_dmad": {}, "slot_mm": {}, "wd_cast": {},
                  "wd_dmad": {}, "down_evac": {}, "down_pe": {}, "fuse_done": {},
                  "datmp_hist": {}}
            b.w("sync", "dve", b.cnt["dve"])
            b.w("sync", "pe", b.cnt["pe"])
            b.w("vector", "pe", b.cnt["pe"])
            ffn_gu_dma(b, st, 0)
            ffn_gu_cast(b, st, 0)
            ffn_wd_dma(b, st, 0)
            ffn_wd_dma(b, st, 1)
            return st

        def emit_ffn(b, st, apply_done):
            slot_mm = st["slot_mm"]
            fuse_done = st["fuse_done"]
            datmp_hist = st["datmp_hist"]
            fuse_dve = None
            wd_state = {"next": 2}
            pool_fence = b.c("pool")

            def emit_down(dog, ol):
                pi = dog * 8 + ol
                ds = pi % 2
                ffn_wd_cast(b, st, pi)
                if wd_state["next"] < 64:
                    ffn_wd_dma(b, st, wd_state["next"])
                    wd_state["next"] += 1
                b.w("tensor", "dve", st["wd_cast"][pi])
                b.w("tensor", "dve", fuse_done[dog])
                for dcl in range(2):
                    dc = 2 * ol + dcl
                    for th in range(NTH):
                        bk = pool6.acquire()
                        for o2 in range(8):
                            b.do("tensor", lambda e, t=bk.get("tf", bk["t"]), ds=ds,
                                 dcl=dcl, o2=o2, th=th, dog=dog, st_=(o2 == 0),
                                 sp=(o2 == 7): e.matmul(
                                     t[:, :],
                                     wdb[ds][:, (dcl * 8 + o2) * 128:
                                             (dcl * 8 + o2) * 128 + 128],
                                     fuse[:, ((dog % 2) * 8 + o2) * T + th * 512:
                                          ((dog % 2) * 8 + o2) * T + th * 512 + 512],
                                     start=st_, stop=sp),
                                 [("pe", 1)] if o2 == 7 else [])
                        b.w("scalar", "pe", b.c("pe"))
                        b.w("scalar", "dve", datmp_hist.get(th))
                        b.do("scalar", lambda e, t=bk.get("tf", bk["t"]), dc=dc, th=th:
                             e.activation(datmp[th][:, :], t[:, :], AF.Copy,
                                          scale=s_sdh[:, dc:dc + 1]), [("act", 1)])
                        pool6.release(bk, "act")
                        b.w("vector", "act", b.c("act"))
                        r = b.do("vector", lambda e, dc=dc, th=th: e.tensor_add(
                            xs(dc, th), datmp[th][:, :], xs(dc, th)), [("dve", 1)])
                        datmp_hist[th] = r["dve"]
                        st["down_evac"][(dc, th)] = r["dve"]
                slot_mm[("d", ds)] = b.c("pe")
                st["down_pe"][dog] = b.c("pe")

            for og in range(9):
                for ol in range(8):
                    if og < 8:
                        oc = og * 8 + ol
                        slot = oc % 2
                        fsl = (og % 2) * 8 + ol
                        if oc + 1 < 64:
                            ffn_gu_dma(b, st, oc + 1)
                        b.w("tensor", "dve", st["gu_cast"][oc])
                        gbanks, ubanks = [], []
                        for th in range(NTH):
                            bk = pool6.acquire()
                            if oc == 0 and apply_done:
                                b.w("tensor", "dve", apply_done[th])
                            for hc in range(NDC):
                                b.do("tensor", lambda e, t=bk.get("tf", bk["t"]), s=slot,
                                     hc=hc, th=th, st_=(hc == 0),
                                     sp=(hc == NDC - 1): e.matmul(
                                         t[:, :], wgbf[s][:, hc * 128:(hc + 1) * 128],
                                         xn[:, hc * T + th * 512:
                                            hc * T + th * 512 + 512],
                                         start=st_, stop=sp),
                                     [("pe", 1)] if hc == NDC - 1 else [])
                            bk["done"] = b.c("pe")
                            gbanks.append(bk)
                        slot_mm[("g", slot)] = b.c("pe")
                        for th in range(NTH):
                            gb = gbanks[th]
                            if oc == 0:
                                b.w("scalar", "pool", pool_fence)
                                if apply_done:
                                    b.w("scalar", "dve", max(apply_done.values()))
                            if fuse_dve is not None:
                                b.w("scalar", "dve", fuse_dve)
                            b.w("scalar", "pe", gb["done"])
                            b.do("scalar", lambda e, t=gb.get("tf", gb["t"]), th=th,
                                 oc=oc: e.activation(
                                     gsl[th][:, :], t[:, :], AF.Silu,
                                     scale=s_sg[:, oc:oc + 1]), [("act", 1)])
                            pool6.release(gb, "act")
                        if og >= 1:
                            emit_down(og - 1, ol)
                        for th in range(NTH):
                            bk = pool6.acquire()
                            for hc in range(NDC):
                                b.do("tensor", lambda e, t=bk.get("tf", bk["t"]), s=slot,
                                     hc=hc, th=th, st_=(hc == 0),
                                     sp=(hc == NDC - 1): e.matmul(
                                         t[:, :], wubf[s][:, hc * 128:(hc + 1) * 128],
                                         xn[:, hc * T + th * 512:
                                            hc * T + th * 512 + 512],
                                         start=st_, stop=sp),
                                     [("pe", 1)] if hc == NDC - 1 else [])
                            bk["done"] = b.c("pe")
                            ubanks.append(bk)
                        slot_mm[("u", slot)] = b.c("pe")
                        if oc + 1 < 64:
                            ffn_gu_cast(b, st, oc + 1)
                        for th in range(NTH):
                            ub = ubanks[th]
                            b.w("scalar", "pe", ub["done"])
                            b.do("scalar", lambda e, t=ub.get("tf", ub["t"]), th=th,
                                 oc=oc: e.activation(
                                     ubuf[th][:, :], t[:, :], AF.Copy,
                                     scale=s_su[:, oc:oc + 1]), [("act", 1)])
                            pool6.release(ub, "act")
                            b.w("vector", "act", b.c("act"))
                            b.w("vector", "pe", st["down_pe"].get(og - 2))
                            r = b.do("vector", lambda e, th=th, fsl=fsl: e.tensor_mul(
                                fuse[:, fsl * T + th * 512: fsl * T + th * 512 + 512],
                                gsl[th][:, :], ubuf[th][:, :]), [("dve", 1)])
                            fuse_dve = r["dve"]
                        if ol == 7:
                            fuse_done[og] = b.c("dve")
                    if og == 8:
                        emit_down(og - 1, ol)

        def emit_proj(b, wsrc, dst, act_fn, st32, stb, transpose_to=None,
                      act_scale=None, pl=None):
            pl = pl or pool
            """dst[oc-part, t] = act(rn @ W^T). If transpose_to is given, the
            activated [e,t] halves are routed through og_st and PE-transposed
            into transpose_to ([t-part, e] blocks at tc*D + oc*128)."""
            b.w("sync", "act", b.cnt["act"])
            b.w("scalar", "pe", b.cnt["pe"])
            cast_done = {}
            ps_dmad = {}
            lastmm = {}

            def ps_dma(oc):
                if oc > 1:
                    b.w("sync", "act", cast_done[oc - 2])
                sem = "s_ps" if oc % 2 == 0 else "s_ps1"
                b.do("sync", lambda e, oc=oc, st=st32[oc % 2]: e.dma_start(
                    out=st[:, :], in_=wsrc.ap()[oc, :, :, :].rearrange("p dc w -> p (dc w)")),
                    [(sem, 16)])
                ps_dmad[oc] = (sem, b.c(sem))

            def ps_cast(oc):
                slot = oc % 2
                b.w("scalar", ps_dmad[oc][0], ps_dmad[oc][1])
                b.w("scalar", "pe", lastmm.get(slot))
                b.do("scalar", lambda e, s=slot, st=st32[oc % 2], stb=stb: e.copy(
                    stb[s][:, :], st[:, :]), [("act", 1)])
                cast_done[oc] = b.c("act")

            pend = []
            tbi = [0]

            def flush_tr(unit):
                oc_, th_, osl_, actc = unit
                tb = ptb[tbi[0] % 2]
                tbi[0] += 1
                b.w("tensor", "act", actc)
                tb_wait(tb)
                for s in range(4):
                    b.do("tensor", lambda e, t=tb["t"], s=s, osl=osl_: e.transpose(
                        t[:, s * 128:(s + 1) * 128],
                        og_st[osl_][:, s * 128:(s + 1) * 128], s_id[:, :]),
                        [("pe", 1)] if s == 3 else [])
                lastmm[("og", osl_)] = b.c("pe")
                b.w("vector", "pe", b.c("pe"))
                for s in range(4):
                    tc = th_ * 4 + s
                    b.do("vector", lambda e, t=tb["t"], s=s, tc=tc, oc=oc_:
                         e.tensor_copy(
                             transpose_to[:, tc * D + oc_ * 128: tc * D + oc_ * 128 + 128],
                             t[:, s * 128:(s + 1) * 128]),
                         [("dve", 1)])
                tb_release(tb, "dve")

            ps_dma(0)
            ps_dma(1)
            ps_cast(0)
            for oc in range(16):
                slot = oc % 2
                if oc + 2 < 16:
                    ps_dma(oc + 2)
                if oc + 1 < 16:
                    ps_cast(oc + 1)
                b.w("tensor", "act", cast_done[oc])
                for th in range(NTH):
                    bk = pl.acquire()
                    for dc in range(NDC):
                        b.do("tensor", lambda e, t=bk.get("tf", bk["t"]), s=slot, stb=stb,
                             dc=dc, th=th, st=(dc == 0), sp=(dc == NDC - 1): e.matmul(
                                 t[:, :], stb[s][:, dc * 128:(dc + 1) * 128],
                                 xn[:, dc * T + th * 512: dc * T + th * 512 + 512],
                                 start=st, stop=sp),
                             [("pe", 1)] if dc == NDC - 1 else [])
                    if transpose_to is None:
                        b.w("scalar", "pe", b.c("pe"))
                        b.do("scalar", lambda e, t=bk.get("tf", bk["t"]), dst=dst, oc=oc,
                             th=th: e.activation(
                                 dst[:, oc * T + th * 512: oc * T + th * 512 + 512],
                                 t[:, :], act_fn), [("act", 1)])
                        pl.release(bk, "act")
                    else:
                        mm_done = b.c("pe")
                        if pend:
                            flush_tr(pend.pop(0))
                        osl = th % 2
                        b.w("scalar", "pe", mm_done)
                        b.w("scalar", "pe", lastmm.get(("og", osl)))
                        if act_scale is None:
                            b.do("scalar", lambda e, t=bk["t"], osl=osl: e.activation(
                                og_st[osl][:, :], t[:, :], act_fn), [("act", 1)])
                        else:
                            b.do("scalar", lambda e, t=bk["t"], osl=osl: e.activation(
                                og_st[osl][:, :], t[:, :], act_fn,
                                scale=act_scale[:, :1]), [("act", 1)])
                        pool.release(bk, "act")
                        pend.append((oc, th, osl, b.c("act")))
                lastmm[slot] = b.c("pe")
            while pend:
                flush_tr(pend.pop(0))

        def emit_attn(b, upto=9, x_ready=None):
            for lo, hi in ((0, 8), (8, 16)):
                if x_ready:
                    sems = {s for (dc, th), (s, v) in x_ready.items() if lo <= dc < hi}
                    for sname in sorted(sems):
                        b.w("scalar", sname,
                            max(v for (dc, th), (s, v) in x_ready.items()
                                if s == sname and lo <= dc < hi))
                b.do("scalar", lambda e, lo=lo, hi=hi: e.dma_start(
                    out=park.ap().rearrange("(dc p) t -> p dc t", p=128)[:, lo:hi, :],
                    in_=x.ap().rearrange("p (dc t) -> p dc t", t=T)[:, lo:hi, :]),
                    [("s_park", 16)])
            park_done = b.c("s_park")
            emit_ln(b, kind=2, x_ready=x_ready)
            b.barrier()
            if upto <= 1:
                return
            emit_proj(b, wkT, kbuf, AF.Silu, pstage32, pstageb, pl=pool6)
            emit_proj(b, wvT, vbuf, AF.Silu, pstage32, pstageb, pl=pool6)
            b.barrier()
            if upto <= 2:
                return
            b.w("vector", "s_park", park_done)
            pairs = [("v", ec, jc) for jc in range(NJC) for ec in range(NDC)]
            pairs += [("k", dc, jc) for jc in range(NJC) for dc in range(NDC)]
            for bi in range(0, len(pairs), 4):
                tb = ptb[(bi // 4) % 2]
                tb_wait(tb)
                for s in range(4):
                    kind, cc_, jc = pairs[bi + s]
                    srcb = vbuf if kind == "v" else kbuf
                    b.do("tensor", lambda e, t=tb["t"], s=s, srcb=srcb, cc_=cc_, jc=jc:
                         e.transpose(t[:, s * 128:(s + 1) * 128],
                                     srcb[:, cc_ * T + jc * 128: cc_ * T + jc * 128 + 128],
                                     s_id[:, :]),
                         [("pe", 1)] if s == 3 else [])
                b.w("vector", "pe", b.c("pe"))
                for s in range(4):
                    kind, cc_, jc = pairs[bi + s]
                    dstt = v_tm if kind == "v" else k_tm
                    b.do("vector", lambda e, t=tb["t"], s=s, dstt=dstt, cc_=cc_, jc=jc:
                         e.tensor_copy(dstt[:, jc * D + cc_ * 128: jc * D + cc_ * 128 + 128],
                                       t[:, s * 128:(s + 1) * 128]),
                         [("dve", 1)])
                tb_release(tb, "dve")
            b.barrier()
            if upto <= 3:
                return
            for er in range(NER):
                b.w("scalar", "s_bo", b.c("s_bo"))
                for dc in range(NDC):
                    bk = pool6.acquire()
                    for jc in range(NJC):
                        b.do("tensor", lambda e, t=bk.get("tf", bk["t"]), jc=jc, dc=dc, er=er,
                             st=(jc == 0), sp=(jc == NJC - 1): e.matmul(
                                 t[:, :], k_tm[:, jc * D + dc * 128: jc * D + dc * 128 + 128],
                                 v_tm[:, jc * D + er * 512: jc * D + er * 512 + 512],
                                 start=st, stop=sp),
                             [("pe", 1)] if jc == NJC - 1 else [])
                    b.w("scalar", "pe", b.c("pe"))
                    b.do("scalar", lambda e, t=bk.get("tf", bk["t"]), dc=dc: e.activation(
                        bstage[:, dc * 512:(dc + 1) * 512], t[:, :], AF.Copy,
                        scale=s_bf[:, :1]), [("act", 1)])
                    pool6.release(bk, "act")
                b.w("scalar", "act", b.c("act"))
                for half in range(2):
                    b.do("scalar", lambda e, er=er, half=half: e.dma_start(
                        out=kvc.ap().rearrange("(dc p) e -> p dc e", p=128)
                            [:, half * 8:(half + 1) * 8, er * 512:(er + 1) * 512],
                        in_=bstage.ap().rearrange("p (dc e) -> p dc e", e=512)
                            [:, half * 8:(half + 1) * 8, :]), [("s_bo", 16)])
            b.w("gpsimd", "s_bo", b.c("s_bo"))
            if USE_CC:
                b.do("gpsimd", lambda e: e.collective_compute(
                    "AllReduce", ALU.add,
                    replica_groups=[[0, 1], [2, 3], [4, 5], [6, 7]],
                    ins=[kvc.ap().opt()], outs=[kvs.ap().opt()]), [("cc", 1)])
            else:
                b.do("gpsimd", lambda e: e.dma_start(out=kvs.ap(), in_=kvc.ap()),
                     [("cc", 16)])
            b.barrier()
            if upto <= 4:
                return
            b.w("sync", "s_bo", b.c("s_bo"))
            emit_proj(b, wqT, qbuf, AF.Silu, qg32, qgb, pl=pool6)
            b.barrier()
            if upto <= 5:
                return
            emit_proj(b, wgT, None, AF.Tanh, qg32, qgb, transpose_to=g_tm,
                      act_scale=s_half)
            b.barrier()
            if upto <= 6:
                return
            for tr in range(NTH):
                for jc in range(0, (tr + 1) * 4):
                    bk = pool6.acquire()
                    for dc in range(NDC):
                        b.do("tensor", lambda e, t=bk.get("tf", bk["t"]), jc=jc, tr=tr, dc=dc,
                             st=(dc == 0), sp=(dc == NDC - 1): e.matmul(
                                 t[:, :], kbuf[:, dc * T + jc * 128: dc * T + jc * 128 + 128],
                                 qbuf[:, dc * T + tr * 512: dc * T + tr * 512 + 512],
                                 start=st, stop=sp),
                             [("pe", 1)] if dc == NDC - 1 else [])
                    pe_g = b.c("pe")
                    lo_tc = tr * 4
                    if jc < lo_tc:
                        b.w("scalar", "pe", pe_g)
                        b.do("scalar", lambda e, t=bk.get("tf", bk["t"]), jc=jc, tr=tr:
                             e.activation(
                                 s_t[:, jc * T + tr * 512: jc * T + tr * 512 + 512], t[:, :],
                                 AF.Copy), [("act", 1)])
                        pool6.release(bk, "act")
                    else:
                        s = jc - lo_tc
                        b.w("vector", "pe", pe_g)
                        b.do("vector", lambda e, t=bk.get("tf", bk["t"]), jc=jc, tr=tr, s=s: e.tensor_mul(
                            s_t[:, jc * T + tr * 512 + s * 128:
                                jc * T + tr * 512 + s * 128 + 128],
                            t[:, s * 128:(s + 1) * 128], s_mask[:, :]),
                            [("dve", 1)])
                        if s < 3:
                            b.do("vector", lambda e, t=bk.get("tf", bk["t"]), jc=jc, tr=tr,
                                 s=s: e.tensor_copy(
                                     s_t[:, jc * T + tr * 512 + (s + 1) * 128:
                                         jc * T + (tr + 1) * 512],
                                     t[:, (s + 1) * 128:512]),
                                 [("dve", 1)])
                        pool6.release(bk, "dve")
            b.barrier()
            if upto <= 7:
                return
            zflag = {}
            apply_mm_done = {}

            def z_load(er):
                zs = er % 2
                if er >= 2:
                    b.w("scalar", "pe", apply_mm_done[er - 2])
                b.w("scalar", "cc", 1)
                for half in range(2):
                    b.do("scalar", lambda e, er=er, zs=zs, half=half: e.dma_start(
                        out=zbuf[zs].ap().rearrange("p (dc e) -> p dc e", e=512)
                            [:, half * 8:(half + 1) * 8, :],
                        in_=kvs.ap().rearrange("(dc p) e -> p dc e", p=128)
                            [:, half * 8:(half + 1) * 8, er * 512:(er + 1) * 512]),
                        [(f"s_z{zs}", 16)])
                b.w("vector", f"s_z{zs}", b.c(f"s_z{zs}"))
                if er >= 2:
                    b.w("vector", "pe", apply_mm_done[er - 2])
                b.do("vector", lambda e, zs=zs: e.tensor_scalar_mul(
                    zbuf[zs][:, :], zbuf[zs][:, :], s_af[:, :1]), [("dve", 1)])
                zflag[er] = b.c("dve")

            z_load(0)
            z_load(1)
            og_pend = []
            og_hist = {}
            og_tbi = [0]

            def og_flush(unit):
                er_, tc_, sl_, dvec = unit
                tb = ptb[og_tbi[0] % 2]
                og_tbi[0] += 1
                b.w("tensor", "dve", dvec)
                tb_wait(tb)
                for s in range(4):
                    b.do("tensor", lambda e, t=tb["t"], s=s, sl=sl_: e.transpose(
                        t[:, s * 128:(s + 1) * 128], og_st[sl_][:, s * 128:(s + 1) * 128],
                        s_id[:, :]), [("pe", 1)] if s == 3 else [])
                og_hist[sl_] = b.c("pe")
                b.w("vector", "pe", b.c("pe"))
                for s in range(4):
                    ec = er_ * 4 + s
                    b.do("vector", lambda e, t=tb["t"], s=s, ec=ec, tc=tc_:
                         e.tensor_copy(ogfm[:, ec * T + tc_ * 128: ec * T + tc_ * 128 + 128],
                                       t[:, s * 128:(s + 1) * 128]),
                         [("dve", 1)])
                tb_release(tb, "dve")

            for er in range(NER):
                zs = er % 2
                b.w("tensor", "dve", zflag[er])
                for tc in range(NTC):
                    bk = pool.acquire()
                    for jc in range(tc + 1):
                        b.do("tensor", lambda e, t=bk["t"], jc=jc, tc=tc, er=er,
                             st=(jc == 0): e.matmul(
                                 t[:, :], s_t[:, jc * T + tc * 128: jc * T + tc * 128 + 128],
                                 v_tm[:, jc * D + er * 512: jc * D + er * 512 + 512],
                                 start=st, stop=False), [])
                    for dc in range(NDC):
                        b.do("tensor", lambda e, t=bk["t"], dc=dc, tc=tc, zs=zs,
                             sp=(dc == NDC - 1): e.matmul(
                                 t[:, :], qbuf[:, dc * T + tc * 128: dc * T + tc * 128 + 128],
                                 zbuf[zs][:, dc * 512:(dc + 1) * 512],
                                 start=False, stop=sp),
                             [("pe", 1)] if dc == NDC - 1 else [])
                    mm_done = b.c("pe")
                    if og_pend:
                        og_flush(og_pend.pop(0))
                    sl = tc % 2
                    b.w("vector", "pe", mm_done)
                    b.w("vector", "pe", og_hist.get(sl))
                    b.do("vector", lambda e, t=bk["t"], tc=tc, er=er, sl=sl:
                         e.scalar_tensor_tensor(
                             og_st[sl][:, :],
                             g_tm[:, tc * D + er * 512: tc * D + er * 512 + 512],
                             1.0, t[:, :], ALU.add, ALU.mult),
                         [("dve", 1)])
                    pool.release(bk, "dve")
                    og_pend.append((er, tc, sl, b.c("dve")))
                apply_mm_done[er] = b.c("pe")
                if er + 2 < NER:
                    z_load(er + 2)
            while og_pend:
                og_flush(og_pend.pop(0))
            b.barrier()
            if upto <= 8:
                return
            mixer_pe = b.c("pe")
            mixer_dve = b.c("dve")
            b.w("scalar", "pe", mixer_pe)
            b.w("scalar", "dve", mixer_dve)
            for dc in range(NDC):
                sx = "s_xla" if dc < 8 else "s_xlb"
                b.do("scalar", lambda e, dc=dc: e.dma_start(
                    out=x.ap().rearrange("p (dc t) -> p dc t", t=T)[:, dc:dc + 1, :],
                    in_=park.ap().rearrange("(dc p) t -> p dc t", p=128)[:, dc:dc + 1, :]),
                    [(sx, 16)])
            xla_done = b.c("s_xla")
            xlb_done = b.c("s_xlb")
            wp_cast = {}
            wp_dmad = {}
            wp_mm = {}
            wp_evac = {}

            def wp_dma(dc):
                if dc > 1:
                    b.w("sync", "act", wp_cast[dc - 2])
                sem = "s_ps" if dc % 2 == 0 else "s_ps1"
                b.do("sync", lambda e, dc=dc, st=wp32[dc % 2]: e.dma_start(
                    out=st[:, :], in_=wpT.ap()[dc, :, :, :].rearrange("p ec w -> p (ec w)")),
                    [(sem, 16)])
                wp_dmad[dc] = (sem, b.c(sem))

            def wp_castf(dc):
                sl = dc % 2
                b.w("scalar", wp_dmad[dc][0], wp_dmad[dc][1])
                b.w("scalar", "pe", wp_mm.get(sl))
                b.do("scalar", lambda e, sl=sl, st=wp32[dc % 2]: e.activation(
                    wpb[sl][:, :], st[:, :], AF.Copy, scale=s_half[:, :1]),
                    [("act", 1)])
                wp_cast[dc] = b.c("act")

            wp_dma(0)
            wp_dma(1)
            wp_castf(0)
            for dc in range(NDC):
                sl = dc % 2
                if dc + 2 < NDC:
                    wp_dma(dc + 2)
                if dc + 1 < NDC:
                    wp_castf(dc + 1)
                b.w("tensor", "act", wp_cast[dc])
                for th in range(NTH):
                    bk = pool6.acquire()
                    for ec in range(NDC):
                        b.do("tensor", lambda e, t=bk.get("tf", bk["t"]), sl=sl, ec=ec, th=th,
                             st=(ec == 0), sp=(ec == NDC - 1): e.matmul(
                                 t[:, :], wpb[sl][:, ec * 128:(ec + 1) * 128],
                                 ogfm[:, ec * T + th * 512: ec * T + th * 512 + 512],
                                 start=st, stop=sp),
                             [("pe", 1)] if ec == NDC - 1 else [])
                    b.w("vector", "pe", b.c("pe"))
                    b.w("vector", "s_xla" if dc < 8 else "s_xlb",
                        xla_done if dc < 8 else xlb_done)
                    r = b.do("vector", lambda e, t=bk.get("tf", bk["t"]), dc=dc, th=th:
                             e.tensor_add(xs(dc, th), t[:, :], xs(dc, th)), [("dve", 1)])
                    wp_evac[(dc, th)] = ("dve", r["dve"])
                    pool6.release(bk, "dve")
                wp_mm[sl] = b.c("pe")
            return wp_evac

        def program(b):
            b.do("sync", lambda e: e.dma_start(
                out=x.ap().rearrange("p (dc t) -> p dc t", t=T)[:, 0:8, :],
                in_=xT.ap().rearrange("(dc p) t -> p dc t", p=128)[:, 0:8, :]), [("s_x", 16)])
            b.do("sync", lambda e: e.dma_start(
                out=x.ap().rearrange("p (dc t) -> p dc t", t=T)[:, 8:16, :],
                in_=xT.ap().rearrange("(dc p) t -> p dc t", p=128)[:, 8:16, :]), [("s_x2", 16)])
            b.do("sync", lambda e: e.dma_start(out=s_sg[:, :], in_=sgate.ap()), [("s_pre", 16)])
            b.do("sync", lambda e: e.dma_start(out=s_su[:, :], in_=sup.ap()), [("s_pre", 16)])
            b.do("sync", lambda e: e.dma_start(out=s_sdh[:, :], in_=sdown.ap()), [("s_pre", 16)])
            b.do("sync", lambda e: e.dma_start(out=s_id[:, :], in_=ident_in.ap()), [("s_pre", 16)])
            b.do("sync", lambda e: e.dma_start(out=s_mask[:, :], in_=mask_in.ap()), [("s_pre", 16)])
            b.do("sync", lambda e: e.dma_start(out=s_bf[:, :], in_=bflag_in.ap()), [("s_pre", 16)])
            b.do("sync", lambda e: e.dma_start(out=s_af[:, :], in_=aflag_in.ap()), [("s_pre", 16)])
            for eng in ("vector", "scalar", "tensor", "gpsimd"):
                b.w(eng, "s_pre", b.c("s_pre"))
            b.do("vector", lambda e: e.memset(ones_c[:, :], 1.0), [("dve", 1)])
            b.do("vector", lambda e: e.memset(ones_r[:, :], 1.0), [("dve", 1)])
            b.do("vector", lambda e: e.memset(c_e1b[:, :], EPS_LN), [("dve", 1)])
            b.do("vector", lambda e: e.memset(c_e1s[:, :], 1.0), [("dve", 1)])
            b.do("vector", lambda e: e.memset(c_e2b[:, :], EPS_F32 * EPS_LN), [("dve", 1)])
            b.do("vector", lambda e: e.memset(c_e2s[:, :], 1.0 + EPS_F32), [("dve", 1)])
            b.do("vector", lambda e: e.tensor_scalar_mul(s_sdh[:, :], s_sdh[:, :], 0.5),
                 [("dve", 1)])
            b.do("vector", lambda e: e.memset(s_half[:, :], 0.5), [("dve", 1)])
            b.do("vector", lambda e: e.memset(ones_cb[:, :], 1.0), [("dve", 1)])
            b.barrier()

            xr = {(dc, th): ("s_x" if dc < 8 else "s_x2", 16)
                  for dc in range(NDC) for th in range(NTH)}
            if "ffn1" in stages:
                st1 = ffn_prolog(b)
                ad = emit_ln(b, kind=1, x_ready=xr)
                emit_ffn(b, st1, ad)
                xr = {k: ("dve", v) for k, v in st1["down_evac"].items()}
            if "attn" in stages:
                xr = emit_attn(b, upto=ATTN_UPTO, x_ready=xr)
            if "ffn2" in stages:
                st2 = ffn_prolog(b)
                ad = emit_ln(b, kind=1, x_ready=xr)
                emit_ffn(b, st2, ad)
                xr = {k: ("dve", v) for k, v in st2["down_evac"].items()}

            h0 = max(v for (dc, th), (s, v) in xr.items() if dc < 8) if xr else None
            h1 = max(v for (dc, th), (s, v) in xr.items() if dc >= 8) if xr else None
            b.w("gpsimd", "dve", h0)
            b.do("gpsimd", lambda e: e.dma_start(
                out=xout.ap().rearrange("(dc p) t -> p dc t", p=128)[:, 0:8, :],
                in_=x.ap().rearrange("p (dc t) -> p dc t", t=T)[:, 0:8, :]), [("st", 16)])
            b.w("gpsimd", "dve", h1)
            b.do("gpsimd", lambda e: e.dma_start(
                out=xout.ap().rearrange("(dc p) t -> p dc t", p=128)[:, 8:16, :],
                in_=x.ap().rearrange("p (dc t) -> p dc t", t=T)[:, 8:16, :]), [("st", 16)])
            b.w("gpsimd", "st", b.c("st"))

        def emit(engname, eng):
            b.begin_pass(engname, eng)
            pool.reset()
            pool6.idx = 0
            for tb_ in ptb:
                tb_["rel"] = []
            prow["R0"] = []
            prow["R1"] = []
            sq_mm_hist.clear()
            xb_hist.clear()
            atmp_hist.clear()
            program(b)

        blk.sync(lambda e: emit("sync", e))
        blk.tensor(lambda e: emit("tensor", e))
        blk.vector(lambda e: emit("vector", e))
        blk.scalar(lambda e: emit("scalar", e))
        blk.gpsimd(lambda e: emit("gpsimd", e))

    return nc


_CACHE = {}


def _get_nc(stages):
    key = tuple(stages)
    if key not in _CACHE:
        _CACHE[key] = build_program(stages)
    return _CACHE[key]


def _prep_host(L0_attn_x, L0_attn_Wq, L0_attn_Wk, L0_attn_Wv, L0_attn_Wg, L0_attn_Wp,
               L0_ffn_w_gate, L0_ffn_w_up, L0_ffn_w_down, L0_ffn_s_gate, L0_ffn_s_up,
               L0_ffn_s_down):
    def lhsT_tiles(w):
        return np.ascontiguousarray(
            np.asarray(w, np.float32).reshape(16, 128, 16, 128).transpose(0, 3, 2, 1))

    shared = {
        "wqT": lhsT_tiles(L0_attn_Wq),
        "wkT": lhsT_tiles(L0_attn_Wk),
        "wvT": lhsT_tiles(L0_attn_Wv),
        "wgT": lhsT_tiles(L0_attn_Wg),
        "wpT": lhsT_tiles(L0_attn_Wp),
        "wgate": np.ascontiguousarray(
            np.asarray(L0_ffn_w_gate).reshape(16, 128, 64, 128).transpose(2, 1, 0, 3)),
        "wup": np.ascontiguousarray(
            np.asarray(L0_ffn_w_up).reshape(16, 128, 64, 128).transpose(2, 1, 0, 3)),
        "wdown": np.ascontiguousarray(
            np.asarray(L0_ffn_w_down).reshape(8, 8, 128, 16, 128).transpose(0, 3, 2, 1, 4)),
        "sgate": np.ascontiguousarray(np.asarray(L0_ffn_s_gate, np.float32).reshape(64, 128).T),
        "sup": np.ascontiguousarray(np.asarray(L0_ffn_s_up, np.float32).reshape(64, 128).T),
        "sdown": np.ascontiguousarray(np.asarray(L0_ffn_s_down, np.float32).reshape(16, 128).T),
        "ident": np.eye(128, dtype=ml_dtypes.bfloat16),
        "mask": np.ascontiguousarray(np.triu(np.ones((128, 128), np.float32))),
    }
    xf = np.asarray(L0_attn_x, np.float32)
    in_maps = []
    for c in range(8):
        bb, hh = c // 2, c % 2
        m = dict(shared)
        m["xT"] = np.ascontiguousarray(xf[bb, hh * T:(hh + 1) * T, :].T)
        even = 1.0 if hh == 0 else 0.0
        m["bflag"] = np.full((128, 1), even, np.float32)
        m["aflag"] = np.full((128, 1), 1.0 - even, np.float32)
        in_maps.append(m)
    return in_maps


def kernel(**inputs):
    in_maps = _prep_host(**inputs)
    nc = _get_nc(STAGES)
    res = run_bass_kernel_spmd(nc, in_maps, core_ids=list(range(8)))
    out = np.empty((4, 2048, 2048), np.float32)
    for c in range(8):
        bb, hh = c // 2, c % 2
        out[bb, hh * T:(hh + 1) * T, :] = res.results[c]["xout"].T
    return out


# version 15 (final)
# speedup vs baseline: 1.1909x; 1.0027x over previous
import numpy as np
import ml_dtypes
import concourse.bass as bass
import concourse.mybir as mybir
from concourse.bass_utils import run_bass_kernel_spmd

F32 = mybir.dt.float32
BF16 = mybir.dt.bfloat16
I8 = mybir.dt.int8
AF = mybir.ActivationFunctionType
ALU = mybir.AluOpType

T = 1024
D = 2048
NDC = 16
NOG = 4
NTH = 2
NTC = 8
NJC = 8
NER = 4

EPS_LN = 1e-5
EPS_F32 = float(np.finfo(np.float32).eps)

STAGES = ("ffn1", "attn", "ffn2")
USE_CC = True
ATTN_UPTO = 9

BASE = 16512
R1 = BASE
R2 = BASE + 65536
R3 = BASE + 98304
R4 = BASE + 131072
R5 = BASE + 163840
H = BASE + 196608
SBUF_TOP = 229344


class B:

    def __init__(self, nc):
        self.nc = nc
        self.cur = None
        self.e = None
        self.sems = {}
        self.cnt = {}

    def sem_init(self, names, es):
        for n in names:
            self.sems[n] = es.enter_context(self.nc.semaphore(n))

    def begin_pass(self, eng, handle):
        self.cur = eng
        self.e = handle
        for n in self.sems:
            self.cnt[n] = 0
        self.hwm = {}

    def w(self, eng, sname, val):
        if val is None or val <= 0:
            return
        key = (eng, sname)
        if self.hwm.get(key, -1) >= val:
            return
        self.hwm[key] = val
        if self.cur == eng:
            self.e.wait_ge(self.sems[sname], val)

    def do(self, eng, fn, incs=()):
        out = {}
        if self.cur == eng:
            ins = fn(self.e)
            for s, n in incs:
                ins = ins.then_inc(self.sems[s], n)
        for s, n in incs:
            self.cnt[s] += n
            out[s] = self.cnt[s]
        return out

    def c(self, sname):
        return self.cnt[sname]

    def barrier(self):
        snap = {s: self.cnt[s] for s in ("pe", "dve", "act", "pool")}
        m = {"tensor": "pe", "vector": "dve", "scalar": "act", "gpsimd": "pool"}
        for eng, own in m.items():
            for o_sem in snap:
                if o_sem != own:
                    self.w(eng, o_sem, snap[o_sem])
        for o_sem in snap:
            self.w("sync", o_sem, snap[o_sem])


class Pool:

    def __init__(self, b, banks):
        self.b = b
        self.banks = [{"t": t, "rel": []} for t in banks]
        self.idx = 0

    def reset(self):
        self.idx = 0
        for bk in self.banks:
            bk["rel"] = []

    def acquire(self):
        bk = self.banks[self.idx % len(self.banks)]
        self.idx += 1
        for sem, cntv in bk["rel"]:
            self.b.w("tensor", sem, cntv)
        return bk

    def release(self, bk, *sems):
        bk["rel"] = [(s, self.b.cnt[s]) for s in sems]


def build_program(stages, use_cc=None):
    global USE_CC
    if use_cc is not None:
        USE_CC = use_cc
    nc = bass.Bass("TRN2", target_bir_lowering=False, debug=False)

    xT = nc.dram_tensor("xT", [D, T], F32, kind="ExternalInput")
    wqT = nc.dram_tensor("wqT", [16, 128, 16, 128], F32, kind="ExternalInput")
    wkT = nc.dram_tensor("wkT", [16, 128, 16, 128], F32, kind="ExternalInput")
    wvT = nc.dram_tensor("wvT", [16, 128, 16, 128], F32, kind="ExternalInput")
    wgT = nc.dram_tensor("wgT", [16, 128, 16, 128], F32, kind="ExternalInput")
    wpT = nc.dram_tensor("wpT", [16, 128, 16, 128], F32, kind="ExternalInput")
    wgate = nc.dram_tensor("wgate", [64, 128, 16, 128], I8, kind="ExternalInput")
    wup = nc.dram_tensor("wup", [64, 128, 16, 128], I8, kind="ExternalInput")
    wdown = nc.dram_tensor("wdown", [8, 16, 128, 8, 128], I8, kind="ExternalInput")
    sgate = nc.dram_tensor("sgate", [128, 64], F32, kind="ExternalInput")
    sup = nc.dram_tensor("sup", [128, 64], F32, kind="ExternalInput")
    sdown = nc.dram_tensor("sdown", [128, 16], F32, kind="ExternalInput")
    ident_in = nc.dram_tensor("ident", [128, 128], BF16, kind="ExternalInput")
    mask_in = nc.dram_tensor("mask", [128, 128], F32, kind="ExternalInput")
    bflag_in = nc.dram_tensor("bflag", [128, 1], F32, kind="ExternalInput")
    aflag_in = nc.dram_tensor("aflag", [128, 1], F32, kind="ExternalInput")
    xout = nc.dram_tensor("xout", [D, T], F32, kind="ExternalOutput")

    park = nc.dram_tensor("park", [D, T], F32)
    kvc = nc.dram_tensor("kvc", [D, D], BF16)
    kvs = nc.dram_tensor("kvs", [D, D], BF16)

    A = lambda name, shape, dt, off: nc.alloc_sbuf_tensor_at(name, shape, dt, offset=off)
    x = A("x", [128, NDC * T], F32, R1)
    pstage32 = [A("pstage32a", [128, 2048], F32, R5),
                A("pstage32b", [128, 2048], F32, R5 + 16384)]
    pstageb = [A(f"pstageb{i}", [128, 2048], BF16, R5 + 8192 + i * 4096) for i in range(2)]
    v_tm = A("v_tm", [128, NJC * D], BF16, R1)
    g_tm = A("g_tm", [128, NTC * D], BF16, R1 + 32768)
    xn = A("xn", [128, NDC * T], BF16, R2)
    s_t = A("s_t", [128, NJC * T], BF16, R2)
    og_st = [A(f"og_st{i}", [128, 512], BF16, H + 12320 + i * 1024) for i in range(2)]
    xbuf = og_st + [A("xbuf2", [128, 512], BF16, H + 14432)]
    assert H + 15456 <= SBUF_TOP
    s_half = A("s_half", [128, 1], F32, H + 14368)
    ones_cb = A("ones_cb", [128, 1], BF16, H + 14400)
    wp32 = [A("wp32a", [128, 2048], F32, R2),
            A("wp32b", [128, 2048], F32, R2 + 16384)]
    wpb = [A(f"wpb{i}", [128, 2048], BF16, R2 + 8192 + i * 4096) for i in range(2)]
    fuse = A("fuse", [128, 16 * T], BF16, R3)
    kbuf = A("kbuf", [128, NDC * T], BF16, R3)
    zbuf = [A(f"zbuf{i}", [128, 16 * 512], BF16, R3 + i * 16384) for i in range(2)]
    vbuf = A("vbuf", [128, NDC * T], BF16, R4)
    wd8 = [A(f"wd8{i}", [128, 2048], I8, R4 + i * 2048) for i in range(2)]
    wdb = [A(f"wdb{i}", [128, 2048], BF16, R4 + 4096 + i * 4096) for i in range(2)]
    gsl = [A(f"gsl{i}", [128, 512], BF16, R4 + 12288 + i * 1024) for i in range(2)]
    ubuf = [A(f"ubuf{i}", [128, 512], BF16, R4 + 14336 + i * 1024) for i in range(2)]
    lnsig = [A(f"lnsig{i}", [128, 512], F32, R4 + 4096 + i * 2048) for i in range(2)]
    atmp = [A(f"atmp{i}", [128, 512], F32, R4 + 12288 + i * 2048) for i in range(2)]
    bstage = A("bstage", [128, 16 * 512], BF16, R4)
    qg32 = [A("qg32a", [128, 2048], F32, R4 + 16384),
            A("qg32b", [128, 2048], F32, R4 + 8192)]
    datmp = [A(f"datmp{i}", [128, 512], F32, R4 + 16384 + i * 2048) for i in range(2)]
    qgb = [A(f"qgb{i}", [128, 2048], BF16, R4 + 24576 + i * 4096) for i in range(2)]
    ogfm = A("ogfm", [128, 16 * T], BF16, R4)
    wg8 = A("wg8", [128, 2048], I8, R5)
    wu8 = A("wu8", [128, 2048], I8, R5 + 2048)
    wgbf = [A(f"wgbf{i}", [128, 2048], BF16, R5 + 4096 + i * 4096) for i in range(2)]
    wubf = [A(f"wubf{i}", [128, 2048], BF16, R5 + 12288 + i * 4096) for i in range(2)]
    k_tm = A("k_tm", [128, NJC * D], BF16, R5)
    qbuf = A("qbuf", [128, NDC * T], BF16, R5)
    sq = [A(f"sq{i}", [128, 512], BF16, H + i * 1024) for i in range(4)]
    row_m = A("row_m", [1, 512], F32, H + 4096)
    row_s = A("row_s", [1, 512], F32, H + 6144)
    row_ms = A("row_ms", [1, 512], F32, H + 8192)
    s_sg = A("s_sg", [128, 64], F32, H + 10240)
    s_su = A("s_su", [128, 64], F32, H + 10496)
    s_sdh = A("s_sdh", [128, 16], F32, H + 10752)
    s_id = A("s_id", [128, 128], BF16, H + 10816)
    s_mask = A("s_mask", [128, 128], F32, H + 11072)
    s_bf = A("s_bf", [128, 1], F32, H + 11584)
    s_af = A("s_af", [128, 1], F32, H + 11616)
    ones_c = A("ones_c", [128, 1], F32, H + 11648)
    ones_r = A("ones_r", [1, 128], F32, H + 11680)
    c_e1b = A("c_e1b", [1, 1], F32, H + 12192)
    c_e1s = A("c_e1s", [1, 1], F32, H + 12224)
    c_e2b = A("c_e2b", [1, 1], F32, H + 12256)
    c_e2s = A("c_e2s", [1, 1], F32, H + 12288)
    assert H + 14432 <= SBUF_TOP

    from contextlib import ExitStack
    es = ExitStack()
    with es:
        pA = es.enter_context(nc.psum_tensor("pA", [128, 512], F32))
        pB = es.enter_context(nc.psum_tensor("pB", [128, 512], F32))
        pC = es.enter_context(nc.psum_tensor("pC", [128, 512], F32))
        pD = es.enter_context(nc.psum_tensor("pD", [128, 512], F32))
        pT0 = es.enter_context(nc.psum_tensor("pT0", [128, 1024], BF16))
        pT1 = es.enter_context(nc.psum_tensor("pT1", [128, 1024], BF16))
        pT0f = pT0.bitcast(F32)
        pT1f = pT1.bitcast(F32)
        pR0 = es.enter_context(nc.psum_tensor("pR0", [1, 512], F32))
        pR1 = es.enter_context(nc.psum_tensor("pR1", [1, 512], F32))

        b = B(nc)
        b.sem_init(["s_pre", "s_x", "s_x2", "s_wg", "s_wu", "s_wd0", "s_wd1", "s_ps", "s_ps1",
                    "s_park", "s_bo", "s_z0", "s_z1", "s_xla", "s_xlb",
                    "pe", "dve", "act", "pool", "cc", "st"], es)
        pool = Pool(b, [pA, pB, pC, pD])
        tbank0 = {"t": pT0, "tf": pT0f, "rel": []}
        tbank1 = {"t": pT1, "tf": pT1f, "rel": []}
        ptb = [tbank0, tbank1]
        pool6 = Pool(b, [])
        pool6.banks = pool.banks + [tbank0, tbank1]
        prow = {"R0": [], "R1": []}

        def tb_wait(tb_):
            for sem, cntv in tb_["rel"]:
                b.w("tensor", sem, cntv)

        def tb_release(tb_, *sems):
            tb_["rel"] = [(s, b.cnt[s]) for s in sems]

        blk = es.enter_context(nc.Block())

        def xs(dc, th):
            return x[:, dc * T + th * 512: dc * T + th * 512 + 512]

        sq_mm_hist = {}
        xb_hist = {}
        atmp_hist = {}

        def emit_ln(b, kind, x_ready=None):
            bias_sl = c_e2b if kind == 2 else c_e1b
            scale_sl = c_e2s if kind == 2 else c_e1s
            apply_done = {}
            tgt = {}
            for th in range(NTH):
                if th == 0:
                    for sem, cntv in prow["R0"] + prow["R1"]:
                        b.w("tensor", sem, cntv)
                    tgt[0] = [None, None, pR0, pR1, 0]
                else:
                    bk1 = pool.acquire()
                    bk2 = pool.acquire()
                    tgt[1] = [bk1, bk2, bk1["t"], bk2["t"], 0]
                for dc in range(NDC):
                    sl4 = (th * NDC + dc) % 3
                    if x_ready:
                        b.w("gpsimd", *x_ready[(dc, th)])
                        b.w("scalar", *x_ready[(dc, th)])
                    b.w("gpsimd", "pe", xb_hist.get(sl4))
                    rc = b.do("gpsimd", lambda e, dc=dc, th=th, sl4=sl4: e.tensor_copy(
                        xbuf[sl4][:, :], xs(dc, th)), [("pool", 1)])
                    b.w("gpsimd", "pe", sq_mm_hist.get(sl4))
                    rq = b.do("gpsimd", lambda e, dc=dc, th=th, sl4=sl4: e.tensor_mul(
                        sq[sl4][:, :], xs(dc, th), xs(dc, th)), [("pool", 1)])
                    b.w("tensor", "pool", rc["pool"])
                    r1 = b.do("tensor", lambda e, t=tgt[th][2], sl4=sl4, st=(dc == 0),
                              sp=(dc == NDC - 1): e.matmul(
                                  t[:1, :], ones_cb[:, :1], xbuf[sl4][:, :],
                                  start=st, stop=sp), [("pe", 1)])
                    xb_hist[sl4] = r1["pe"]
                    b.w("tensor", "pool", rq["pool"])
                    r2 = b.do("tensor", lambda e, t=tgt[th][3], sl4=sl4, st=(dc == 0),
                              sp=(dc == NDC - 1): e.matmul(
                                  t[:1, :], ones_cb[:, :1], sq[sl4][:, :],
                                  start=st, stop=sp), [("pe", 1)])
                    sq_mm_hist[sl4] = r2["pe"]
                tgt[th][4] = b.c("pe")
            for th in range(NTH):
                bk1, bk2, tS, tQ, stats_pe = tgt[th]
                b.w("vector", "pe", stats_pe)
                b.do("vector", lambda e, tS=tS: e.tensor_scalar_mul(
                    row_m[:, :], tS[:1, :], 1.0 / D), [("dve", 1)])
                if th == 0:
                    prow["R0"] = [("dve", b.cnt["dve"])]
                b.do("vector", lambda e, tQ=tQ: e.tensor_scalar_mul(
                    row_s[:, :], tQ[:1, :], 1.0 / D), [("dve", 1)])
                if th == 0:
                    prow["R1"] = [("dve", b.cnt["dve"])]
                else:
                    pool.release(bk1, "dve")
                    pool.release(bk2, "dve")
                b.w("vector", "dve", b.c("dve"))
                b.do("vector", lambda e: e.tensor_mul(row_ms[:, :], row_m[:, :], row_m[:, :]),
                     [("dve", 1)])
                b.w("vector", "dve", b.c("dve"))
                b.do("vector", lambda e: e.tensor_sub(row_s[:, :], row_s[:, :], row_ms[:, :]),
                     [("dve", 1)])
                b.w("scalar", "dve", b.c("dve"))
                b.do("scalar", lambda e: e.activation(
                    row_ms[:, :], row_s[:, :], AF.Sqrt, bias=bias_sl[:1, :1],
                    scale=scale_sl[:1, :1]), [("act", 1)])
                b.w("vector", "act", b.c("act"))
                b.do("vector", lambda e: e.reciprocal(row_s[:, :], row_ms[:, :]), [("dve", 1)])
                b.w("vector", "dve", b.c("dve"))
                b.do("vector", lambda e: e.tensor_mul(row_ms[:, :], row_m[:, :], row_s[:, :]),
                     [("dve", 1)])
                bkS = pool.acquire()
                b.w("tensor", "dve", b.c("dve"))
                b.do("tensor", lambda e, t=bkS["t"]: e.matmul(
                    t[:, :], ones_r[:1, :], row_s[:1, :], start=True, stop=True), [("pe", 1)])
                bkM = pool.acquire()
                b.do("tensor", lambda e, t=bkM["t"]: e.matmul(
                    t[:, :], ones_r[:1, :], row_ms[:1, :], start=True, stop=True), [("pe", 1)])
                b.w("scalar", "pe", b.c("pe"))
                b.do("scalar", lambda e, th=th, t=bkS["t"]: e.activation(
                    lnsig[th % 2][:, :], t[:, :], AF.Copy), [("act", 1)])
                pool.release(bkS, "act")
                sig_ready = b.c("act")
                b.w("gpsimd", "act", sig_ready)
                b.w("vector", "pe", b.c("pe"))
                for dc in range(NDC):
                    if x_ready:
                        b.w("gpsimd", *x_ready[(dc, th)])
                    b.w("gpsimd", "dve", atmp_hist.get(dc % 2))
                    r1 = b.do("gpsimd", lambda e, dc=dc, th=th: e.tensor_mul(
                        atmp[dc % 2][:, :], xs(dc, th), lnsig[th % 2][:, :]), [("pool", 1)])
                    b.w("vector", "pool", r1["pool"])
                    r2 = b.do("vector", lambda e, dc=dc, th=th, m=bkM["t"]:
                              e.tensor_sub(xn[:, dc * T + th * 512: dc * T + th * 512 + 512],
                                           atmp[dc % 2][:, :], m[:, :]), [("dve", 1)])
                    atmp_hist[dc % 2] = r2["dve"]
                apply_done[th] = b.c("dve")
                pool.release(bkM, "dve")
            return apply_done

        def ffn_gu_dma(b, st, oc):
            if oc > 0:
                b.w("sync", "dve", st["gu_cast"][oc - 1])
            b.do("sync", lambda e, oc=oc: e.dma_start(
                out=wg8[:, :],
                in_=wgate.ap()[oc, :, :, :].rearrange("p hc w -> p (hc w)")),
                [("s_wg", 16)])
            b.do("sync", lambda e, oc=oc: e.dma_start(
                out=wu8[:, :],
                in_=wup.ap()[oc, :, :, :].rearrange("p hc w -> p (hc w)")),
                [("s_wu", 16)])
            st["gu_dmad"][oc] = (b.c("s_wg"), b.c("s_wu"))

        def ffn_gu_cast(b, st, oc):
            slot = oc % 2
            b.w("vector", "s_wg", st["gu_dmad"][oc][0])
            b.w("vector", "pe", st["slot_mm"].get(("g", slot)))
            b.do("vector", lambda e, s=slot: e.tensor_copy(wgbf[s][:, :], wg8[:, :]),
                 [("dve", 1)])
            b.w("vector", "s_wu", st["gu_dmad"][oc][1])
            b.w("vector", "pe", st["slot_mm"].get(("u", slot)))
            b.do("vector", lambda e, s=slot: e.tensor_copy(wubf[s][:, :], wu8[:, :]),
                 [("dve", 1)])
            st["gu_cast"][oc] = b.c("dve")

        def ffn_wd_dma(b, st, pi):
            og, k = pi // 8, pi % 8
            ds = pi % 2
            if pi >= 2:
                b.w("sync", "dve", st["wd_cast"][pi - 2])
            b.do("sync", lambda e, og=og, k=k, ds=ds: e.dma_start(
                out=wd8[ds].ap().rearrange("p (dcl ol w) -> p dcl ol w", dcl=2, ol=8),
                in_=wdown.ap()[og, 2 * k: 2 * k + 2, :, :, :].rearrange(
                    "dcl p ol w -> p dcl ol w")), [(f"s_wd{ds}", 16)])
            st["wd_dmad"][pi] = b.c(f"s_wd{ds}")

        def ffn_wd_cast(b, st, pi):
            ds = pi % 2
            b.w("vector", f"s_wd{ds}", st["wd_dmad"][pi])
            b.w("vector", "pe", st["slot_mm"].get(("d", ds)))
            b.do("vector", lambda e, ds=ds: e.tensor_copy(wdb[ds][:, :], wd8[ds][:, :]),
                 [("dve", 1)])
            st["wd_cast"][pi] = b.c("dve")

        def ffn_prolog(b):
            st = {"gu_cast": {}, "gu_dmad": {}, "slot_mm": {}, "wd_cast": {},
                  "wd_dmad": {}, "down_evac": {}, "down_pe": {}, "fuse_done": {},
                  "datmp_hist": {}}
            b.w("sync", "dve", b.cnt["dve"])
            b.w("sync", "pe", b.cnt["pe"])
            b.w("vector", "pe", b.cnt["pe"])
            ffn_gu_dma(b, st, 0)
            ffn_gu_cast(b, st, 0)
            ffn_wd_dma(b, st, 0)
            ffn_wd_dma(b, st, 1)
            return st

        def emit_ffn(b, st, apply_done):
            slot_mm = st["slot_mm"]
            fuse_done = st["fuse_done"]
            datmp_hist = st["datmp_hist"]
            fuse_dve = None
            wd_state = {"next": 2}
            pool_fence = b.c("pool")

            def emit_down(dog, ol):
                pi = dog * 8 + ol
                ds = pi % 2
                if pi not in st["wd_cast"]:
                    ffn_wd_cast(b, st, pi)
                if wd_state["next"] < 64:
                    ffn_wd_dma(b, st, wd_state["next"])
                    wd_state["next"] += 1
                b.w("tensor", "dve", st["wd_cast"][pi])
                b.w("tensor", "dve", fuse_done[dog])
                for dcl in range(2):
                    dc = 2 * ol + dcl
                    for th in range(NTH):
                        bk = pool6.acquire()
                        for o2 in range(8):
                            b.do("tensor", lambda e, t=bk.get("tf", bk["t"]), ds=ds,
                                 dcl=dcl, o2=o2, th=th, dog=dog, st_=(o2 == 0),
                                 sp=(o2 == 7): e.matmul(
                                     t[:, :],
                                     wdb[ds][:, (dcl * 8 + o2) * 128:
                                             (dcl * 8 + o2) * 128 + 128],
                                     fuse[:, ((dog % 2) * 8 + o2) * T + th * 512:
                                          ((dog % 2) * 8 + o2) * T + th * 512 + 512],
                                     start=st_, stop=sp),
                                 [("pe", 1)] if o2 == 7 else [])
                        b.w("scalar", "pe", b.c("pe"))
                        b.w("scalar", "dve", datmp_hist.get(th))
                        b.do("scalar", lambda e, t=bk.get("tf", bk["t"]), dc=dc, th=th:
                             e.activation(datmp[th][:, :], t[:, :], AF.Copy,
                                          scale=s_sdh[:, dc:dc + 1]), [("act", 1)])
                        pool6.release(bk, "act")
                        b.w("vector", "act", b.c("act"))
                        r = b.do("vector", lambda e, dc=dc, th=th: e.tensor_add(
                            xs(dc, th), datmp[th][:, :], xs(dc, th)), [("dve", 1)])
                        datmp_hist[th] = r["dve"]
                        st["down_evac"][(dc, th)] = r["dve"]
                slot_mm[("d", ds)] = b.c("pe")
                st["down_pe"][dog] = b.c("pe")
                if pi + 1 < 64 and (pi + 1) in st["wd_dmad"]:
                    ffn_wd_cast(b, st, pi + 1)

            for og in range(9):
                for ol in range(8):
                    if og < 8:
                        oc = og * 8 + ol
                        slot = oc % 2
                        fsl = (og % 2) * 8 + ol
                        if oc + 1 < 64:
                            ffn_gu_dma(b, st, oc + 1)
                        b.w("tensor", "dve", st["gu_cast"][oc])
                        gbanks, ubanks = [], []
                        for th in range(NTH):
                            bk = pool6.acquire()
                            if oc == 0 and apply_done:
                                b.w("tensor", "dve", apply_done[th])
                            for hc in range(NDC):
                                b.do("tensor", lambda e, t=bk.get("tf", bk["t"]), s=slot,
                                     hc=hc, th=th, st_=(hc == 0),
                                     sp=(hc == NDC - 1): e.matmul(
                                         t[:, :], wgbf[s][:, hc * 128:(hc + 1) * 128],
                                         xn[:, hc * T + th * 512:
                                            hc * T + th * 512 + 512],
                                         start=st_, stop=sp),
                                     [("pe", 1)] if hc == NDC - 1 else [])
                            bk["done"] = b.c("pe")
                            gbanks.append(bk)
                        slot_mm[("g", slot)] = b.c("pe")
                        for th in range(NTH):
                            gb = gbanks[th]
                            if oc == 0:
                                b.w("scalar", "pool", pool_fence)
                                if apply_done:
                                    b.w("scalar", "dve", max(apply_done.values()))
                            if fuse_dve is not None:
                                b.w("scalar", "dve", fuse_dve)
                            b.w("scalar", "pe", gb["done"])
                            b.do("scalar", lambda e, t=gb.get("tf", gb["t"]), th=th,
                                 oc=oc: e.activation(
                                     gsl[th][:, :], t[:, :], AF.Silu,
                                     scale=s_sg[:, oc:oc + 1]), [("act", 1)])
                            pool6.release(gb, "act")
                        if og >= 1:
                            emit_down(og - 1, ol)
                        for th in range(NTH):
                            bk = pool6.acquire()
                            for hc in range(NDC):
                                b.do("tensor", lambda e, t=bk.get("tf", bk["t"]), s=slot,
                                     hc=hc, th=th, st_=(hc == 0),
                                     sp=(hc == NDC - 1): e.matmul(
                                         t[:, :], wubf[s][:, hc * 128:(hc + 1) * 128],
                                         xn[:, hc * T + th * 512:
                                            hc * T + th * 512 + 512],
                                         start=st_, stop=sp),
                                     [("pe", 1)] if hc == NDC - 1 else [])
                            bk["done"] = b.c("pe")
                            ubanks.append(bk)
                        slot_mm[("u", slot)] = b.c("pe")
                        if oc + 1 < 64:
                            ffn_gu_cast(b, st, oc + 1)
                        for th in range(NTH):
                            ub = ubanks[th]
                            b.w("scalar", "pe", ub["done"])
                            b.do("scalar", lambda e, t=ub.get("tf", ub["t"]), th=th,
                                 oc=oc: e.activation(
                                     ubuf[th][:, :], t[:, :], AF.Copy,
                                     scale=s_su[:, oc:oc + 1]), [("act", 1)])
                            pool6.release(ub, "act")
                            b.w("vector", "act", b.c("act"))
                            b.w("vector", "pe", st["down_pe"].get(og - 2))
                            r = b.do("vector", lambda e, th=th, fsl=fsl: e.tensor_mul(
                                fuse[:, fsl * T + th * 512: fsl * T + th * 512 + 512],
                                gsl[th][:, :], ubuf[th][:, :]), [("dve", 1)])
                            fuse_dve = r["dve"]
                        if ol == 7:
                            fuse_done[og] = b.c("dve")
                    if og == 8:
                        emit_down(og - 1, ol)

        def emit_proj(b, wsrc, dst, act_fn, st32, stb, transpose_to=None,
                      act_scale=None, pl=None):
            pl = pl or pool
            """dst[oc-part, t] = act(rn @ W^T). If transpose_to is given, the
            activated [e,t] halves are routed through og_st and PE-transposed
            into transpose_to ([t-part, e] blocks at tc*D + oc*128)."""
            b.w("sync", "act", b.cnt["act"])
            b.w("scalar", "pe", b.cnt["pe"])
            cast_done = {}
            ps_dmad = {}
            lastmm = {}

            def ps_dma(oc):
                if oc > 1:
                    b.w("sync", "act", cast_done[oc - 2])
                sem = "s_ps" if oc % 2 == 0 else "s_ps1"
                b.do("sync", lambda e, oc=oc, st=st32[oc % 2]: e.dma_start(
                    out=st[:, :], in_=wsrc.ap()[oc, :, :, :].rearrange("p dc w -> p (dc w)")),
                    [(sem, 16)])
                ps_dmad[oc] = (sem, b.c(sem))

            def ps_cast(oc):
                slot = oc % 2
                b.w("scalar", ps_dmad[oc][0], ps_dmad[oc][1])
                b.w("scalar", "pe", lastmm.get(slot))
                b.do("scalar", lambda e, s=slot, st=st32[oc % 2], stb=stb: e.copy(
                    stb[s][:, :], st[:, :]), [("act", 1)])
                cast_done[oc] = b.c("act")

            pend = []
            tbi = [0]

            def flush_tr(unit):
                oc_, th_, osl_, actc = unit
                tb = ptb[tbi[0] % 2]
                tbi[0] += 1
                b.w("tensor", "act", actc)
                tb_wait(tb)
                for s in range(4):
                    b.do("tensor", lambda e, t=tb["t"], s=s, osl=osl_: e.transpose(
                        t[:, s * 128:(s + 1) * 128],
                        og_st[osl_][:, s * 128:(s + 1) * 128], s_id[:, :]),
                        [("pe", 1)] if s == 3 else [])
                lastmm[("og", osl_)] = b.c("pe")
                b.w("vector", "pe", b.c("pe"))
                for s in range(4):
                    tc = th_ * 4 + s
                    b.do("vector", lambda e, t=tb["t"], s=s, tc=tc, oc=oc_:
                         e.tensor_copy(
                             transpose_to[:, tc * D + oc_ * 128: tc * D + oc_ * 128 + 128],
                             t[:, s * 128:(s + 1) * 128]),
                         [("dve", 1)])
                tb_release(tb, "dve")

            ps_dma(0)
            ps_dma(1)
            ps_cast(0)
            for oc in range(16):
                slot = oc % 2
                if oc + 2 < 16:
                    ps_dma(oc + 2)
                if oc + 1 < 16:
                    ps_cast(oc + 1)
                b.w("tensor", "act", cast_done[oc])
                for th in range(NTH):
                    bk = pl.acquire()
                    for dc in range(NDC):
                        b.do("tensor", lambda e, t=bk.get("tf", bk["t"]), s=slot, stb=stb,
                             dc=dc, th=th, st=(dc == 0), sp=(dc == NDC - 1): e.matmul(
                                 t[:, :], stb[s][:, dc * 128:(dc + 1) * 128],
                                 xn[:, dc * T + th * 512: dc * T + th * 512 + 512],
                                 start=st, stop=sp),
                             [("pe", 1)] if dc == NDC - 1 else [])
                    if transpose_to is None:
                        b.w("scalar", "pe", b.c("pe"))
                        b.do("scalar", lambda e, t=bk.get("tf", bk["t"]), dst=dst, oc=oc,
                             th=th: e.activation(
                                 dst[:, oc * T + th * 512: oc * T + th * 512 + 512],
                                 t[:, :], act_fn), [("act", 1)])
                        pl.release(bk, "act")
                    else:
                        mm_done = b.c("pe")
                        if pend:
                            flush_tr(pend.pop(0))
                        osl = th % 2
                        b.w("scalar", "pe", mm_done)
                        b.w("scalar", "pe", lastmm.get(("og", osl)))
                        if act_scale is None:
                            b.do("scalar", lambda e, t=bk["t"], osl=osl: e.activation(
                                og_st[osl][:, :], t[:, :], act_fn), [("act", 1)])
                        else:
                            b.do("scalar", lambda e, t=bk["t"], osl=osl: e.activation(
                                og_st[osl][:, :], t[:, :], act_fn,
                                scale=act_scale[:, :1]), [("act", 1)])
                        pool.release(bk, "act")
                        pend.append((oc, th, osl, b.c("act")))
                lastmm[slot] = b.c("pe")
            while pend:
                flush_tr(pend.pop(0))

        def emit_attn(b, upto=9, x_ready=None):
            for lo, hi in ((0, 8), (8, 16)):
                if x_ready:
                    sems = {s for (dc, th), (s, v) in x_ready.items() if lo <= dc < hi}
                    for sname in sorted(sems):
                        b.w("scalar", sname,
                            max(v for (dc, th), (s, v) in x_ready.items()
                                if s == sname and lo <= dc < hi))
                b.do("scalar", lambda e, lo=lo, hi=hi: e.dma_start(
                    out=park.ap().rearrange("(dc p) t -> p dc t", p=128)[:, lo:hi, :],
                    in_=x.ap().rearrange("p (dc t) -> p dc t", t=T)[:, lo:hi, :]),
                    [("s_park", 16)])
            park_done = b.c("s_park")
            emit_ln(b, kind=2, x_ready=x_ready)
            b.barrier()
            if upto <= 1:
                return
            emit_proj(b, wkT, kbuf, AF.Silu, pstage32, pstageb, pl=pool6)
            emit_proj(b, wvT, vbuf, AF.Silu, pstage32, pstageb, pl=pool6)
            b.barrier()
            if upto <= 2:
                return
            b.w("vector", "s_park", park_done)
            pairs = [("v", ec, jc) for jc in range(NJC) for ec in range(NDC)]
            pairs += [("k", dc, jc) for jc in range(NJC) for dc in range(NDC)]
            for bi in range(0, len(pairs), 4):
                tb = ptb[(bi // 4) % 2]
                tb_wait(tb)
                for s in range(4):
                    kind, cc_, jc = pairs[bi + s]
                    srcb = vbuf if kind == "v" else kbuf
                    b.do("tensor", lambda e, t=tb["t"], s=s, srcb=srcb, cc_=cc_, jc=jc:
                         e.transpose(t[:, s * 128:(s + 1) * 128],
                                     srcb[:, cc_ * T + jc * 128: cc_ * T + jc * 128 + 128],
                                     s_id[:, :]),
                         [("pe", 1)] if s == 3 else [])
                b.w("vector", "pe", b.c("pe"))
                for s in range(4):
                    kind, cc_, jc = pairs[bi + s]
                    dstt = v_tm if kind == "v" else k_tm
                    b.do("vector", lambda e, t=tb["t"], s=s, dstt=dstt, cc_=cc_, jc=jc:
                         e.tensor_copy(dstt[:, jc * D + cc_ * 128: jc * D + cc_ * 128 + 128],
                                       t[:, s * 128:(s + 1) * 128]),
                         [("dve", 1)])
                tb_release(tb, "dve")
            b.barrier()
            if upto <= 3:
                return
            for er in range(NER):
                b.w("scalar", "s_bo", b.c("s_bo"))
                for dc in range(NDC):
                    bk = pool6.acquire()
                    for jc in range(NJC):
                        b.do("tensor", lambda e, t=bk.get("tf", bk["t"]), jc=jc, dc=dc, er=er,
                             st=(jc == 0), sp=(jc == NJC - 1): e.matmul(
                                 t[:, :], k_tm[:, jc * D + dc * 128: jc * D + dc * 128 + 128],
                                 v_tm[:, jc * D + er * 512: jc * D + er * 512 + 512],
                                 start=st, stop=sp),
                             [("pe", 1)] if jc == NJC - 1 else [])
                    b.w("scalar", "pe", b.c("pe"))
                    b.do("scalar", lambda e, t=bk.get("tf", bk["t"]), dc=dc: e.activation(
                        bstage[:, dc * 512:(dc + 1) * 512], t[:, :], AF.Copy,
                        scale=s_bf[:, :1]), [("act", 1)])
                    pool6.release(bk, "act")
                b.w("scalar", "act", b.c("act"))
                for half in range(2):
                    b.do("scalar", lambda e, er=er, half=half: e.dma_start(
                        out=kvc.ap().rearrange("(dc p) e -> p dc e", p=128)
                            [:, half * 8:(half + 1) * 8, er * 512:(er + 1) * 512],
                        in_=bstage.ap().rearrange("p (dc e) -> p dc e", e=512)
                            [:, half * 8:(half + 1) * 8, :]), [("s_bo", 16)])
            b.w("gpsimd", "s_bo", b.c("s_bo"))
            if USE_CC:
                b.do("gpsimd", lambda e: e.collective_compute(
                    "AllReduce", ALU.add,
                    replica_groups=[[0, 1], [2, 3], [4, 5], [6, 7]],
                    ins=[kvc.ap().opt()], outs=[kvs.ap().opt()]), [("cc", 1)])
            else:
                b.do("gpsimd", lambda e: e.dma_start(out=kvs.ap(), in_=kvc.ap()),
                     [("cc", 16)])
            b.barrier()
            if upto <= 4:
                return
            b.w("sync", "s_bo", b.c("s_bo"))
            emit_proj(b, wqT, qbuf, AF.Silu, qg32, qgb, pl=pool6)
            b.barrier()
            if upto <= 5:
                return
            emit_proj(b, wgT, None, AF.Tanh, qg32, qgb, transpose_to=g_tm,
                      act_scale=s_half)
            b.barrier()
            if upto <= 6:
                return
            for tr in range(NTH):
                for jc in range(0, (tr + 1) * 4):
                    bk = pool6.acquire()
                    for dc in range(NDC):
                        b.do("tensor", lambda e, t=bk.get("tf", bk["t"]), jc=jc, tr=tr, dc=dc,
                             st=(dc == 0), sp=(dc == NDC - 1): e.matmul(
                                 t[:, :], kbuf[:, dc * T + jc * 128: dc * T + jc * 128 + 128],
                                 qbuf[:, dc * T + tr * 512: dc * T + tr * 512 + 512],
                                 start=st, stop=sp),
                             [("pe", 1)] if dc == NDC - 1 else [])
                    pe_g = b.c("pe")
                    lo_tc = tr * 4
                    if jc < lo_tc:
                        b.w("scalar", "pe", pe_g)
                        b.do("scalar", lambda e, t=bk.get("tf", bk["t"]), jc=jc, tr=tr:
                             e.activation(
                                 s_t[:, jc * T + tr * 512: jc * T + tr * 512 + 512], t[:, :],
                                 AF.Copy), [("act", 1)])
                        pool6.release(bk, "act")
                    else:
                        s = jc - lo_tc
                        b.w("vector", "pe", pe_g)
                        b.do("vector", lambda e, t=bk.get("tf", bk["t"]), jc=jc, tr=tr, s=s: e.tensor_mul(
                            s_t[:, jc * T + tr * 512 + s * 128:
                                jc * T + tr * 512 + s * 128 + 128],
                            t[:, s * 128:(s + 1) * 128], s_mask[:, :]),
                            [("dve", 1)])
                        if s < 3:
                            b.do("vector", lambda e, t=bk.get("tf", bk["t"]), jc=jc, tr=tr,
                                 s=s: e.tensor_copy(
                                     s_t[:, jc * T + tr * 512 + (s + 1) * 128:
                                         jc * T + (tr + 1) * 512],
                                     t[:, (s + 1) * 128:512]),
                                 [("dve", 1)])
                        pool6.release(bk, "dve")
            b.barrier()
            if upto <= 7:
                return
            zflag = {}
            apply_mm_done = {}

            def z_load(er):
                zs = er % 2
                if er >= 2:
                    b.w("scalar", "pe", apply_mm_done[er - 2])
                b.w("scalar", "cc", 1)
                for half in range(2):
                    b.do("scalar", lambda e, er=er, zs=zs, half=half: e.dma_start(
                        out=zbuf[zs].ap().rearrange("p (dc e) -> p dc e", e=512)
                            [:, half * 8:(half + 1) * 8, :],
                        in_=kvs.ap().rearrange("(dc p) e -> p dc e", p=128)
                            [:, half * 8:(half + 1) * 8, er * 512:(er + 1) * 512]),
                        [(f"s_z{zs}", 16)])
                b.w("vector", f"s_z{zs}", b.c(f"s_z{zs}"))
                if er >= 2:
                    b.w("vector", "pe", apply_mm_done[er - 2])
                b.do("vector", lambda e, zs=zs: e.tensor_scalar_mul(
                    zbuf[zs][:, :], zbuf[zs][:, :], s_af[:, :1]), [("dve", 1)])
                zflag[er] = b.c("dve")

            z_load(0)
            z_load(1)
            og_pend = []
            og_hist = {}
            og_tbi = [0]

            def og_flush(unit):
                er_, tc_, sl_, dvec = unit
                tb = ptb[og_tbi[0] % 2]
                og_tbi[0] += 1
                b.w("tensor", "dve", dvec)
                tb_wait(tb)
                for s in range(4):
                    b.do("tensor", lambda e, t=tb["t"], s=s, sl=sl_: e.transpose(
                        t[:, s * 128:(s + 1) * 128], og_st[sl_][:, s * 128:(s + 1) * 128],
                        s_id[:, :]), [("pe", 1)] if s == 3 else [])
                og_hist[sl_] = b.c("pe")
                b.w("vector", "pe", b.c("pe"))
                for s in range(4):
                    ec = er_ * 4 + s
                    b.do("vector", lambda e, t=tb["t"], s=s, ec=ec, tc=tc_:
                         e.tensor_copy(ogfm[:, ec * T + tc_ * 128: ec * T + tc_ * 128 + 128],
                                       t[:, s * 128:(s + 1) * 128]),
                         [("dve", 1)])
                tb_release(tb, "dve")

            for er in range(NER):
                zs = er % 2
                b.w("tensor", "dve", zflag[er])
                for tc in range(NTC):
                    bk = pool.acquire()
                    for jc in range(tc + 1):
                        b.do("tensor", lambda e, t=bk["t"], jc=jc, tc=tc, er=er,
                             st=(jc == 0): e.matmul(
                                 t[:, :], s_t[:, jc * T + tc * 128: jc * T + tc * 128 + 128],
                                 v_tm[:, jc * D + er * 512: jc * D + er * 512 + 512],
                                 start=st, stop=False), [])
                    for dc in range(NDC):
                        b.do("tensor", lambda e, t=bk["t"], dc=dc, tc=tc, zs=zs,
                             sp=(dc == NDC - 1): e.matmul(
                                 t[:, :], qbuf[:, dc * T + tc * 128: dc * T + tc * 128 + 128],
                                 zbuf[zs][:, dc * 512:(dc + 1) * 512],
                                 start=False, stop=sp),
                             [("pe", 1)] if dc == NDC - 1 else [])
                    mm_done = b.c("pe")
                    if og_pend:
                        og_flush(og_pend.pop(0))
                    sl = tc % 2
                    b.w("vector", "pe", mm_done)
                    b.w("vector", "pe", og_hist.get(sl))
                    b.do("vector", lambda e, t=bk["t"], tc=tc, er=er, sl=sl:
                         e.scalar_tensor_tensor(
                             og_st[sl][:, :],
                             g_tm[:, tc * D + er * 512: tc * D + er * 512 + 512],
                             1.0, t[:, :], ALU.add, ALU.mult),
                         [("dve", 1)])
                    pool.release(bk, "dve")
                    og_pend.append((er, tc, sl, b.c("dve")))
                apply_mm_done[er] = b.c("pe")
                if er + 2 < NER:
                    z_load(er + 2)
            while og_pend:
                og_flush(og_pend.pop(0))
            b.barrier()
            if upto <= 8:
                return
            mixer_pe = b.c("pe")
            mixer_dve = b.c("dve")
            b.w("scalar", "pe", mixer_pe)
            b.w("scalar", "dve", mixer_dve)
            for dc in range(NDC):
                sx = "s_xla" if dc < 8 else "s_xlb"
                b.do("scalar", lambda e, dc=dc: e.dma_start(
                    out=x.ap().rearrange("p (dc t) -> p dc t", t=T)[:, dc:dc + 1, :],
                    in_=park.ap().rearrange("(dc p) t -> p dc t", p=128)[:, dc:dc + 1, :]),
                    [(sx, 16)])
            xla_done = b.c("s_xla")
            xlb_done = b.c("s_xlb")
            wp_cast = {}
            wp_dmad = {}
            wp_mm = {}
            wp_evac = {}

            def wp_dma(dc):
                if dc > 1:
                    b.w("sync", "act", wp_cast[dc - 2])
                sem = "s_ps" if dc % 2 == 0 else "s_ps1"
                b.do("sync", lambda e, dc=dc, st=wp32[dc % 2]: e.dma_start(
                    out=st[:, :], in_=wpT.ap()[dc, :, :, :].rearrange("p ec w -> p (ec w)")),
                    [(sem, 16)])
                wp_dmad[dc] = (sem, b.c(sem))

            def wp_castf(dc):
                sl = dc % 2
                b.w("scalar", wp_dmad[dc][0], wp_dmad[dc][1])
                b.w("scalar", "pe", wp_mm.get(sl))
                b.do("scalar", lambda e, sl=sl, st=wp32[dc % 2]: e.activation(
                    wpb[sl][:, :], st[:, :], AF.Copy, scale=s_half[:, :1]),
                    [("act", 1)])
                wp_cast[dc] = b.c("act")

            wp_dma(0)
            wp_dma(1)
            wp_castf(0)
            for dc in range(NDC):
                sl = dc % 2
                if dc + 2 < NDC:
                    wp_dma(dc + 2)
                if dc + 1 < NDC:
                    wp_castf(dc + 1)
                b.w("tensor", "act", wp_cast[dc])
                for th in range(NTH):
                    bk = pool6.acquire()
                    for ec in range(NDC):
                        b.do("tensor", lambda e, t=bk.get("tf", bk["t"]), sl=sl, ec=ec, th=th,
                             st=(ec == 0), sp=(ec == NDC - 1): e.matmul(
                                 t[:, :], wpb[sl][:, ec * 128:(ec + 1) * 128],
                                 ogfm[:, ec * T + th * 512: ec * T + th * 512 + 512],
                                 start=st, stop=sp),
                             [("pe", 1)] if ec == NDC - 1 else [])
                    b.w("vector", "pe", b.c("pe"))
                    b.w("vector", "s_xla" if dc < 8 else "s_xlb",
                        xla_done if dc < 8 else xlb_done)
                    r = b.do("vector", lambda e, t=bk.get("tf", bk["t"]), dc=dc, th=th:
                             e.tensor_add(xs(dc, th), t[:, :], xs(dc, th)), [("dve", 1)])
                    wp_evac[(dc, th)] = ("dve", r["dve"])
                    pool6.release(bk, "dve")
                wp_mm[sl] = b.c("pe")
            return wp_evac

        def program(b):
            b.do("sync", lambda e: e.dma_start(
                out=x.ap().rearrange("p (dc t) -> p dc t", t=T)[:, 0:8, :],
                in_=xT.ap().rearrange("(dc p) t -> p dc t", p=128)[:, 0:8, :]), [("s_x", 16)])
            b.do("sync", lambda e: e.dma_start(
                out=x.ap().rearrange("p (dc t) -> p dc t", t=T)[:, 8:16, :],
                in_=xT.ap().rearrange("(dc p) t -> p dc t", p=128)[:, 8:16, :]), [("s_x2", 16)])
            b.do("sync", lambda e: e.dma_start(out=s_sg[:, :], in_=sgate.ap()), [("s_pre", 16)])
            b.do("sync", lambda e: e.dma_start(out=s_su[:, :], in_=sup.ap()), [("s_pre", 16)])
            b.do("sync", lambda e: e.dma_start(out=s_sdh[:, :], in_=sdown.ap()), [("s_pre", 16)])
            b.do("sync", lambda e: e.dma_start(out=s_id[:, :], in_=ident_in.ap()), [("s_pre", 16)])
            b.do("sync", lambda e: e.dma_start(out=s_mask[:, :], in_=mask_in.ap()), [("s_pre", 16)])
            b.do("sync", lambda e: e.dma_start(out=s_bf[:, :], in_=bflag_in.ap()), [("s_pre", 16)])
            b.do("sync", lambda e: e.dma_start(out=s_af[:, :], in_=aflag_in.ap()), [("s_pre", 16)])
            for eng in ("vector", "scalar", "tensor", "gpsimd"):
                b.w(eng, "s_pre", b.c("s_pre"))
            b.do("vector", lambda e: e.memset(ones_c[:, :], 1.0), [("dve", 1)])
            b.do("vector", lambda e: e.memset(ones_r[:, :], 1.0), [("dve", 1)])
            b.do("vector", lambda e: e.memset(c_e1b[:, :], EPS_LN), [("dve", 1)])
            b.do("vector", lambda e: e.memset(c_e1s[:, :], 1.0), [("dve", 1)])
            b.do("vector", lambda e: e.memset(c_e2b[:, :], EPS_F32 * EPS_LN), [("dve", 1)])
            b.do("vector", lambda e: e.memset(c_e2s[:, :], 1.0 + EPS_F32), [("dve", 1)])
            b.do("vector", lambda e: e.tensor_scalar_mul(s_sdh[:, :], s_sdh[:, :], 0.5),
                 [("dve", 1)])
            b.do("vector", lambda e: e.memset(s_half[:, :], 0.5), [("dve", 1)])
            b.do("vector", lambda e: e.memset(ones_cb[:, :], 1.0), [("dve", 1)])
            b.barrier()

            xr = {(dc, th): ("s_x" if dc < 8 else "s_x2", 16)
                  for dc in range(NDC) for th in range(NTH)}
            if "ffn1" in stages:
                st1 = ffn_prolog(b)
                ad = emit_ln(b, kind=1, x_ready=xr)
                emit_ffn(b, st1, ad)
                xr = {k: ("dve", v) for k, v in st1["down_evac"].items()}
            if "attn" in stages:
                xr = emit_attn(b, upto=ATTN_UPTO, x_ready=xr)
            if "ffn2" in stages:
                st2 = ffn_prolog(b)
                ad = emit_ln(b, kind=1, x_ready=xr)
                emit_ffn(b, st2, ad)
                xr = {k: ("dve", v) for k, v in st2["down_evac"].items()}

            h0 = max(v for (dc, th), (s, v) in xr.items() if dc < 8) if xr else None
            h1 = max(v for (dc, th), (s, v) in xr.items() if dc >= 8) if xr else None
            b.w("gpsimd", "dve", h0)
            b.do("gpsimd", lambda e: e.dma_start(
                out=xout.ap().rearrange("(dc p) t -> p dc t", p=128)[:, 0:8, :],
                in_=x.ap().rearrange("p (dc t) -> p dc t", t=T)[:, 0:8, :]), [("st", 16)])
            b.w("gpsimd", "dve", h1)
            b.do("gpsimd", lambda e: e.dma_start(
                out=xout.ap().rearrange("(dc p) t -> p dc t", p=128)[:, 8:16, :],
                in_=x.ap().rearrange("p (dc t) -> p dc t", t=T)[:, 8:16, :]), [("st", 16)])
            b.w("gpsimd", "st", b.c("st"))

        def emit(engname, eng):
            b.begin_pass(engname, eng)
            pool.reset()
            pool6.idx = 0
            for tb_ in ptb:
                tb_["rel"] = []
            prow["R0"] = []
            prow["R1"] = []
            sq_mm_hist.clear()
            xb_hist.clear()
            atmp_hist.clear()
            program(b)

        blk.sync(lambda e: emit("sync", e))
        blk.tensor(lambda e: emit("tensor", e))
        blk.vector(lambda e: emit("vector", e))
        blk.scalar(lambda e: emit("scalar", e))
        blk.gpsimd(lambda e: emit("gpsimd", e))

    return nc


_CACHE = {}


def _get_nc(stages):
    key = tuple(stages)
    if key not in _CACHE:
        _CACHE[key] = build_program(stages)
    return _CACHE[key]


def _prep_host(L0_attn_x, L0_attn_Wq, L0_attn_Wk, L0_attn_Wv, L0_attn_Wg, L0_attn_Wp,
               L0_ffn_w_gate, L0_ffn_w_up, L0_ffn_w_down, L0_ffn_s_gate, L0_ffn_s_up,
               L0_ffn_s_down):
    def lhsT_tiles(w):
        return np.ascontiguousarray(
            np.asarray(w, np.float32).reshape(16, 128, 16, 128).transpose(0, 3, 2, 1))

    shared = {
        "wqT": lhsT_tiles(L0_attn_Wq),
        "wkT": lhsT_tiles(L0_attn_Wk),
        "wvT": lhsT_tiles(L0_attn_Wv),
        "wgT": lhsT_tiles(L0_attn_Wg),
        "wpT": lhsT_tiles(L0_attn_Wp),
        "wgate": np.ascontiguousarray(
            np.asarray(L0_ffn_w_gate).reshape(16, 128, 64, 128).transpose(2, 1, 0, 3)),
        "wup": np.ascontiguousarray(
            np.asarray(L0_ffn_w_up).reshape(16, 128, 64, 128).transpose(2, 1, 0, 3)),
        "wdown": np.ascontiguousarray(
            np.asarray(L0_ffn_w_down).reshape(8, 8, 128, 16, 128).transpose(0, 3, 2, 1, 4)),
        "sgate": np.ascontiguousarray(np.asarray(L0_ffn_s_gate, np.float32).reshape(64, 128).T),
        "sup": np.ascontiguousarray(np.asarray(L0_ffn_s_up, np.float32).reshape(64, 128).T),
        "sdown": np.ascontiguousarray(np.asarray(L0_ffn_s_down, np.float32).reshape(16, 128).T),
        "ident": np.eye(128, dtype=ml_dtypes.bfloat16),
        "mask": np.ascontiguousarray(np.triu(np.ones((128, 128), np.float32))),
    }
    xf = np.asarray(L0_attn_x, np.float32)
    in_maps = []
    for c in range(8):
        bb, hh = c // 2, c % 2
        m = dict(shared)
        m["xT"] = np.ascontiguousarray(xf[bb, hh * T:(hh + 1) * T, :].T)
        even = 1.0 if hh == 0 else 0.0
        m["bflag"] = np.full((128, 1), even, np.float32)
        m["aflag"] = np.full((128, 1), 1.0 - even, np.float32)
        in_maps.append(m)
    return in_maps


def kernel(**inputs):
    in_maps = _prep_host(**inputs)
    nc = _get_nc(STAGES)
    res = run_bass_kernel_spmd(nc, in_maps, core_ids=list(range(8)))
    out = np.empty((4, 2048, 2048), np.float32)
    for c in range(8):
        bb, hh = c // 2, c % 2
        out[bb, hh * T:(hh + 1) * T, :] = res.results[c]["xout"].T
    return out
